# version 01
# baseline (speedup 1.0000x reference)
import math
from contextlib import ExitStack
import numpy as np
import concourse.bass as bass
import concourse.mybir as mybir
from concourse.bass_utils import run_bass_kernel_spmd

F32 = mybir.dt.float32
BF16 = mybir.dt.bfloat16
I8 = mybir.dt.int8
I32 = mybir.dt.int32
AX = mybir.AxisListType.X
ALU = mybir.AluOpType
ACTF = mybir.ActivationFunctionType

T, TQ, E, H, DH, FI = 2048, 1024, 2048, 16, 128, 8192
NE, NT, NTQ = E // 128, T // 128, TQ // 128
VB = 130
SCALE = 1.0 / math.sqrt(DH)
EXP_BIAS = -4.0
TWO_PI, INV_2PI = 2 * math.pi, 1.0 / (2 * math.pi)
PI_CLAMP = 3.141592
LN1E6 = math.log(1.0e6)
CKMUL = float(2 ** 21)
EPS = 1e-5
NOPS = 3


def ts(i, n=128):
    return slice(i * n, (i + 1) * n)


def nops(eng, n=NOPS):
    for _ in range(n):
        eng.engine_nop()


class SemPool:

    SIZES = {"c": 17, "s": 6, "g": 5}

    def __init__(self, nc):
        n = sum(self.SIZES.values())
        self.sems = [nc.alloc_semaphore(f"sp{i}") for i in range(n)]
        self.base = [0] * n
        self.free = {}
        k = 0
        for kind, cnt in self.SIZES.items():
            self.free[kind] = list(range(k, k + cnt))
            k += cnt

    def alloc(self, kind="c"):
        i = self.free[kind].pop()
        return SemRef(self, i, kind)

    def release(self, ref):
        self.base[ref.i] += ref.count
        self.free[ref.kind].append(ref.i)


class SemRef:
    def __init__(self, pool, i, kind="c"):
        self.pool, self.i, self.kind = pool, i, kind
        self.h = pool.sems[i]
        self.b = pool.base[i]
        self.count = 0

    def inc(self, inst, k=1):
        inst.then_inc(self.h, k)
        self.count += k
        return inst

    def wait(self, eng, k):
        eng.wait_ge(self.h, self.b + k)


class Phase:

    def __init__(self, pool):
        self.pool = pool
        self.refs = []

    def sem(self, kind="c"):
        r = self.pool.alloc(kind)
        self.refs.append(r)
        return r

    def chain(self):
        return Chain(self.sem())

    def __enter__(self):
        return self

    def __exit__(self, *a):
        for r in self.refs:
            self.pool.release(r)
        return False


class Chain:

    def __init__(self, sref):
        self.s = sref
        self.n = 0

    def cut(self, eng, inst):
        self.n += 1
        inst.then_inc(self.s.h, 1)
        self.s.count += 1
        eng.wait_ge(self.s.h, self.s.b + self.n)
        return inst


def build_nc(expose=(), upto="Z"):
    nc = bass.Bass("TRN2", target_bir_lowering=False)
    PH = "0ABCDEFRGZ"
    lim = PH.index(upto)

    def en(p):
        return PH.index(p) <= lim

    def dram(name, shape, dtype):
        kind = "ExternalOutput" if name in expose else "Internal"
        return nc.dram_tensor(name, shape, dtype, kind=kind).ap()

    _uid = [0]

    def _sbuf(X, nm, sh, dt):
        _uid[0] += 1
        return X.enter_context(nc.sbuf_tensor(f"{nm}_u{_uid[0]}", sh, dt))

    def fm3d(ap2d, rowstride, tcol, width=128, nchunk=NE, rowoff=0):
        return bass.AP(ap2d.tensor, rowoff * rowstride + tcol,
                       [[rowstride, 128], [128 * rowstride, nchunk], [1, width]])

    xkv = nc.dram_tensor("x_kv", [T, E], F32, kind="ExternalInput").ap()
    pos = nc.dram_tensor("pos", [1, T], F32, kind="ExternalInput").ap()
    wT = {}
    for nm in ("wqT", "wkT", "wvT", "woT"):
        wT[nm] = nc.dram_tensor(nm, [E, E], F32, kind="ExternalInput").ap()
    wg8 = nc.dram_tensor("w_gate", [E, FI], I8, kind="ExternalInput").ap()
    wu8 = nc.dram_tensor("w_up", [E, FI], I8, kind="ExternalInput").ap()
    wd8 = nc.dram_tensor("w_down", [FI, E], I8, kind="ExternalInput").ap()
    sgin = nc.dram_tensor("s_gate", [1, FI], F32, kind="ExternalInput").ap()
    suin = nc.dram_tensor("s_up", [1, FI], F32, kind="ExternalInput").ap()
    sdin = nc.dram_tensor("s_down", [1, E], F32, kind="ExternalInput").ap()
    out = nc.dram_tensor("out", [TQ, E], F32, kind="ExternalOutput").ap()

    xlt_hi, xlt_lo = dram("xlt_hi", [E, T], BF16), dram("xlt_lo", [E, T], BF16)
    kfm_hi, kfm_lo = dram("kfm_hi", [E, T], BF16), dram("kfm_lo", [E, T], BF16)
    vtm_hi, vtm_lo = dram("vtm_hi", [T, H * VB], BF16), dram("vtm_lo", [T, H * VB], BF16)
    qfm_hi, qfm_lo = dram("qfm_hi", [E, TQ], BF16), dram("qfm_lo", [E, TQ], BF16)
    atf_hi, atf_lo = dram("atf_hi", [E, TQ], BF16), dram("atf_lo", [E, TQ], BF16)
    aor = dram("aor", [TQ, E], F32)
    x2sp = dram("x2sp", [TQ, E], F32)
    x2lt_hi, x2lt_lo = dram("x2lt_hi", [E, TQ], BF16), dram("x2lt_lo", [E, TQ], BF16)

    ident = nc.alloc_sbuf_tensor("ident", [128, 128], F32)
    cpool = nc.alloc_sbuf_tensor("cpool", [128, 2], F32)
    nc.const_aps.register(F32, EXP_BIAS, cpool[:, 0:1])
    cos_tab = nc.alloc_sbuf_tensor("cos_tab", [128, NT * 128], F32)
    sin_tab = nc.alloc_sbuf_tensor("sin_tab", [128, NT * 128], F32)
    amask = nc.alloc_sbuf_tensor("amask", [128, 4 * 512], F32)
    ones512 = nc.alloc_sbuf_tensor("ones512", [1, 512], BF16)
    ck_bf = nc.alloc_sbuf_tensor("ck_bf", [1, T], BF16)
    postile = nc.alloc_sbuf_tensor("postile", [1, T], F32)
    inv_row = nc.alloc_sbuf_tensor("inv_row", [1, 64], F32)
    sgate_p = nc.alloc_sbuf_tensor("sgate_p", [128, 64], F32)
    sup_p = nc.alloc_sbuf_tensor("sup_p", [128, 64], F32)
    sdown_p = nc.alloc_sbuf_tensor("sdown_p", [128, 16], F32)
    PB = [nc.alloc_psum_tensor(f"pb{i}", [128, 512], F32) for i in range(8)]

    POOL = SemPool(nc)

    def ln_stats_dve(v, st, xin, ch):
        ch.cut(v, v.reduce_sum(st[:, 0:1], xin, axis=AX))
        v.tensor_scalar_mul(st[:, 3:4], st[:, 0:1], 1.0 / E)
        ch.cut(v, v.tensor_scalar_mul(st[:, 2:3], st[:, 1:2], 1.0 / E))
        ch.cut(v, v.tensor_tensor(st[:, 4:5], st[:, 3:4], st[:, 3:4], ALU.mult))
        ch.cut(v, v.tensor_sub(st[:, 4:5], st[:, 2:3], st[:, 4:5]))
        return v.tensor_scalar_add(st[:, 4:5], st[:, 4:5], EPS)

    with Phase(POOL) as P, ExitStack() as X:
        s_gp, s_inv, s_ang, s_red, s_cs = (P.sem() for _ in range(5))
        s_ld, s_scl = P.sem('s'), P.sem('s')
        sb = lambda nm, sh, dt: _sbuf(X, nm, sh, dt)
        jrow = sb("jrow", [1, 64], F32)
        z512 = sb("z512", [128, 512], F32)
        ckrow = sb("ckrow", [1, T], F32)
        rsin = sb("rsin", [128, 1024], F32)
        rcos = sb("rcos", [128, 1024], F32)
        tmpang = sb("tmpang", [128, 1024], F32)
        kint0 = sb("kint0", [128, 1024], I32)
        blk = X.enter_context(nc.Block())

        @blk.sync
        def _(sync):
            s_ld.inc(sync.dma_start(out=postile[:], in_=pos), 16)
            s_scl.inc(sync.dma_start(out=sgate_p[:],
                      in_=sgin.rearrange("one (c p) -> (one p) c", p=128),
                      allow_slow_non_contiguous=True), 16)
            s_scl.inc(sync.dma_start(out=sup_p[:],
                      in_=suin.rearrange("one (c p) -> (one p) c", p=128),
                      allow_slow_non_contiguous=True), 16)
            s_scl.inc(sync.dma_start(out=sdown_p[:],
                      in_=sdin.rearrange("one (c p) -> (one p) c", p=128),
                      allow_slow_non_contiguous=True), 16)
            sync.wait_ge(s_scl.h, s_scl.b + 48)

        @blk.gpsimd
        def _(g):
            g.memset(cpool[:, 0:1], EXP_BIAS)
            g.memset(cpool[:, 1:2], 0.0)
            g.memset(ident[:], 1.0)
            g.memset(ones512[:], 1.0)
            g.memset(z512[:], 0.0)
            g.iota(jrow[:], [[1, 64]], channel_multiplier=0,
                   allow_small_or_imprecise_dtypes=True).then_inc(s_gp.h, 1)
            g.wait_ge(s_gp.h, s_gp.b + 1)
            g.affine_select(ident[:], ident[:], [[-1, 128]], ALU.is_equal, 0.0,
                            channel_multiplier=1)
            for m, base in enumerate((0, 128, 256, 384)):
                g.affine_select(amask[:, ts(m, 512)], z512[:], [[1, 512]],
                                ALU.is_ge, -1.0e9, base=-base, channel_multiplier=-1)
            g.engine_nop().then_inc(s_gp.h, 1)
        s_gp.count += 2

        @blk.scalar
        def _(sc):
            s_gp.wait(sc, 2)
            s_inv.inc(sc.activation(inv_row[:], jrow[:], ACTF.Exp, scale=-LN1E6 / 64.0))
            for b in range(2):
                s_red.wait(sc, b + 1)
                for half in range(2):
                    outap = bass.AP(sin_tab, b * 8 * 128 + half * 64,
                                    [[NT * 128, 128], [128, 8], [1, 64]])
                    sc.activation(outap,
                                  rsin[:, ts(b, 512)].rearrange("p (i c) -> p i c", c=64),
                                  ACTF.Sin)
                s_cs.inc(sc.activation(rcos[:, ts(b, 512)], rsin[:, ts(b, 512)],
                                       ACTF.Sin, scale=0.5))

        @blk.vector
        def _(v):
            ch = P.chain()
            s_ld.wait(v, 16)
            ch.cut(v, v.tensor_scalar_min(ckrow[:], postile[:], 0.0))
            ch.cut(v, v.tensor_scalar_mul(ckrow[:], ckrow[:], CKMUL))
            v.tensor_copy(ck_bf[:], ckrow[:])
            for b in range(2):
                s_ang.wait(v, b + 1)
                vb, cb = rsin[:, ts(b, 512)], rcos[:, ts(b, 512)]
                tb, kb = tmpang[:, ts(b, 512)], kint0[:, ts(b, 512)]
                pbv = PB[b][:]
                ch.cut(v, v.tensor_scalar_mul(tb, pbv, INV_2PI))
                ch.cut(v, v.tensor_copy(kb, tb))
                ch.cut(v, v.tensor_copy(tb, kb))
                ch.cut(v, v.scalar_tensor_tensor(vb, tb, -TWO_PI, pbv, ALU.mult, ALU.add))
                ch.cut(v, v.tensor_single_scalar(tb, vb, math.pi, ALU.is_gt))
                ch.cut(v, v.scalar_tensor_tensor(vb, tb, -TWO_PI, vb, ALU.mult, ALU.add))
                ch.cut(v, v.tensor_single_scalar(tb, vb, -math.pi, ALU.is_lt))
                ch.cut(v, v.scalar_tensor_tensor(vb, tb, TWO_PI, vb, ALU.mult, ALU.add))
                ch.cut(v, v.tensor_scalar_min(vb, vb, PI_CLAMP))
                s_red.inc(v.tensor_scalar_max(vb, vb, -PI_CLAMP))
            for b in range(2):
                s_cs.wait(v, b + 1)
                cb = rcos[:, ts(b, 512)]
                tb = tmpang[:, ts(b, 512)]
                ch.cut(v, v.tensor_tensor(tb, cb, cb, ALU.mult))
                for half in range(2):
                    outap = bass.AP(cos_tab, b * 8 * 128 + half * 64,
                                    [[NT * 128, 128], [128, 8], [1, 64]])
                    inap = bass.AP(tmpang, b * 512,
                                   [[1024, 128], [64, 8], [1, 64]])
                    v.tensor_scalar(outap, inap, -2.0, 1.0, ALU.mult, ALU.add)

        @blk.tensor
        def _(t):
            s_inv.wait(t, 1)
            s_ld.wait(t, 16)
            for b in range(2):
                for i in range(8):
                    tile = b * 8 + i
                    mm = t.matmul(PB[b][:, i * 64:(i + 1) * 64],
                                  postile[0:1, ts(tile)], inv_row[:],
                                  start=(i == 0), stop=(i == 7))
                s_ang.inc(mm)

    def ln_fm_phase(src, dst_hi, dst_lo, ntiles, dststride):
        with Phase(POOL) as P, ExitStack() as X:
            s_sq, s_var, s_sd, s_xl, s_tp, s_hi, s_ev = (P.sem() for _ in range(7))
            s_x = [P.sem('s'), P.sem('s')]
            s_sp = [P.sem('g'), P.sem('g')]
            sb = lambda nm, sh, dt: _sbuf(X, nm, sh, dt)
            xrow = [sb(f"xrow{b}", [128, E], F32) for b in range(2)]
            scrap = sb("scrap", [128, E], F32)
            lstat = [sb(f"lstat{b}", [128, 8], F32) for b in range(2)]
            xl = [sb(f"xl{b}", [128, E], F32) for b in range(2)]
            th = [sb(f"th{b}", [128, E], BF16) for b in range(2)]
            tl = [sb(f"tl{b}", [128, E], BF16) for b in range(2)]
            blk = X.enter_context(nc.Block())

            @blk.sync
            def _(sync):
                for i in range(ntiles):
                    if i >= 2:
                        s_xl.wait(sync, i - 1)
                    s_x[i % 2].inc(sync.dma_start(out=xrow[i % 2][:], in_=src[ts(i), :]), 16)

            @blk.gpsimd
            def _(g):
                for i in range(ntiles):
                    s_ev.wait(g, 4 * (i + 1))
                    s_sp[i % 2].inc(g.dma_start(
                        out=fm3d(dst_hi, dststride, i * 128),
                        in_=th[i % 2].rearrange("p (j c) -> p j c", c=128)), 16)
                    s_sp[i % 2].inc(g.dma_start(
                        out=fm3d(dst_lo, dststride, i * 128),
                        in_=tl[i % 2].rearrange("p (j c) -> p j c", c=128)), 16)
                s_sp[0].wait(g, 32 * ((ntiles + 1) // 2))
                s_sp[1].wait(g, 32 * (ntiles // 2))

            @blk.scalar
            def _(sc):
                for i in range(ntiles):
                    st = lstat[i % 2]
                    s_x[i % 2].wait(sc, 16 * (i // 2 + 1))
                    s_sq.inc(sc.activation(scrap[:], xrow[i % 2][:], ACTF.Square,
                                           accum_out=st[:, 1:2]))
                    s_var.wait(sc, i + 1)
                    s_sd.inc(sc.activation(st[:, 5:6], st[:, 4:5], ACTF.Sqrt))
                    for b in range(4):
                        s_tp.wait(sc, 4 * i + b + 1)
                        if i >= 2:
                            s_sp[i % 2].wait(sc, 32 * ((i - 2) // 2 + 1))
                        s_hi.inc(sc.activation(th[i % 2][:, ts(b, 512)], PB[b][:],
                                               ACTF.Copy))

            @blk.vector
            def _(v):
                ch = P.chain()
                for i in range(ntiles):
                    st = lstat[i % 2]
                    s_sq.wait(v, i + 1)
                    s_var.inc(ln_stats_dve(v, st, xrow[i % 2][:], ch))
                    s_sd.wait(v, i + 1)
                    ch.cut(v, v.reciprocal(st[:, 6:7], st[:, 5:6]))
                    if i >= 2:
                        s_tp.wait(v, 4 * (i - 1))
                    s_xl.inc(v.tensor_scalar(xl[i % 2][:], xrow[i % 2][:], st[:, 3:4],
                                             st[:, 6:7], ALU.subtract, ALU.mult))
                    for b in range(4):
                        s_hi.wait(v, 4 * i + b + 1)
                        s_ev.inc(v.tensor_sub(tl[i % 2][:, ts(b, 512)], PB[b][:],
                                              th[i % 2][:, ts(b, 512)]))

            @blk.tensor
            def _(t):
                for i in range(ntiles):
                    s_xl.wait(t, i + 1)
                    if i >= 1:
                        s_ev.wait(t, 4 * i)
                    for j in range(NE):
                        b = j // 4
                        mm = t.matmul(PB[b][:, ts(j % 4)], xl[i % 2][:, ts(j)], ident[:],
                                      is_transpose=True, start=(j % 4 == 0), stop=(j % 4 == 3))
                        if j % 4 == 3:
                            s_tp.inc(mm)

    if en("A"):
        ln_fm_phase(xkv, xlt_hi, xlt_lo, NT, T)

    def proj_phase(wsrc, mode, dst_hi, dst_lo, tiles, fh, taboff=0):
        rope = mode in ("kfm", "qfm")
        dststride = T if mode == "kfm" else TQ
        HW = 1024
        NH = 8 if mode != "vtm" else 8
        ohw = 8 * VB if mode == "vtm" else HW
        with Phase(POOL) as P, ExitStack() as X:
            s_whi, s_wk, s_ps, s_rp, s_tp, s_hi, s_ev, s_vm = (P.sem() for _ in range(8))
            s_wld = [P.sem('s'), P.sem('s')]
            s_a = [P.sem('s'), P.sem('s')]
            s_sp = [P.sem('g'), P.sem('g')]
            sb = lambda nm, sh, dt: _sbuf(X, nm, sh, dt)
            wf32 = [sb(f"wf32{b}", [128, HW], F32) for b in range(2)]
            w_hi = sb("w_hi", [128, NE * HW], BF16)
            w_lo = sb("w_lo", [128, NE * HW], BF16)
            a_hi = [sb(f"a_hi{b}", [128, E], BF16) for b in range(2)]
            a_lo = [sb(f"a_lo{b}", [128, E], BF16) for b in range(2)]
            prj = [sb(f"prj{b}", [128, HW], F32) for b in range(2)]
            rtmp = sb("rtmp", [128, HW], F32)
            oh = [sb(f"oh{b}", [128, ohw], BF16) for b in range(2)]
            ol = [sb(f"ol{b}", [128, ohw], BF16) for b in range(2)]
            blk = X.enter_context(nc.Block())

            def vap(buf, coloff, cnt, nh=8):
                return bass.AP(buf, coloff, [[ohw, 128], [VB, nh], [1, cnt]])

            def vap2(buf, fc):
                return bass.AP(buf, (fc * 4) * VB, [[ohw, 128], [VB, 4], [1, 128]])

            @blk.sync
            def _(sync):
                for j in range(NE):
                    if j >= 2:
                        s_wk.wait(sync, j - 1)
                    s_wld[j % 2].inc(sync.dma_start(out=wf32[j % 2][:],
                                                    in_=wT[wsrc][ts(j), ts(fh, 1024)]), 16)
                for n, i in enumerate(tiles):
                    if n >= 2:
                        s_ps.wait(sync, 2 * (n - 1))
                    s_a[n % 2].inc(sync.dma_start(
                        out=a_hi[n % 2].rearrange("p (j c) -> p j c", c=128),
                        in_=fm3d(xlt_hi, T, i * 128)), 16)
                    s_a[n % 2].inc(sync.dma_start(
                        out=a_lo[n % 2].rearrange("p (j c) -> p j c", c=128),
                        in_=fm3d(xlt_lo, T, i * 128)), 16)

            @blk.gpsimd
            def _(g):
                if mode == "vtm":
                    for buf in range(2):
                        g.memset(vap(oh[buf], 128, 1), 1.0)
                        g.memset(vap(oh[buf], 129, 1), 0.0)
                        lastms = g.memset(vap(ol[buf], 128, 2), 0.0)
                        s_vm.inc(lastms)
                    g.wait_ge(s_vm.h, s_vm.b + 2)
                for n, i in enumerate(tiles):
                    s_ev.wait(g, 2 * (n + 1))
                    if mode == "vtm":
                        s_sp[n % 2].inc(g.dma_start(out=dst_hi[ts(n), ts(fh, 8 * VB)],
                                                    in_=oh[n % 2][:]), 16)
                        s_sp[n % 2].inc(g.dma_start(out=dst_lo[ts(n), ts(fh, 8 * VB)],
                                                    in_=ol[n % 2][:]), 16)
                    else:
                        tc = (i * 128) if mode == "kfm" else ((i - 8) * 128)
                        s_sp[n % 2].inc(g.dma_start(
                            out=fm3d(dst_hi, dststride, tc, nchunk=8, rowoff=fh * 1024),
                            in_=oh[n % 2].rearrange("p (j c) -> p j c", c=128)), 16)
                        s_sp[n % 2].inc(g.dma_start(
                            out=fm3d(dst_lo, dststride, tc, nchunk=8, rowoff=fh * 1024),
                            in_=ol[n % 2].rearrange("p (j c) -> p j c", c=128)), 16)
                s_sp[0].wait(g, 32 * ((len(tiles) + 1) // 2))
                s_sp[1].wait(g, 32 * (len(tiles) // 2))

            @blk.scalar
            def _(sc):
                for j in range(NE):
                    s_wld[j % 2].wait(sc, 16 * (j // 2 + 1))
                    s_whi.inc(sc.activation(w_hi[:, j * HW:(j + 1) * HW], wf32[j % 2][:],
                                            ACTF.Copy))
                for n, i in enumerate(tiles):
                    if rope:
                        for grp in range(2):
                            s_tp.wait(sc, 2 * n + grp + 1)
                            if n >= 2:
                                s_sp[n % 2].wait(sc, 32 * ((n - 2) // 2 + 1))
                            s_hi.inc(sc.activation(oh[n % 2][:, ts(grp, 512)], PB[grp][:],
                                                   ACTF.Copy))
                    else:
                        for fc in range(2):
                            s_ps.wait(sc, 2 * n + fc + 1)
                            if n >= 2:
                                s_sp[n % 2].wait(sc, 32 * ((n - 2) // 2 + 1))
                            s_hi.inc(sc.activation(
                                vap2(oh[n % 2], fc),
                                PB[4 + fc][:].rearrange("p (h c) -> p h c", c=128),
                                ACTF.Copy))

            @blk.vector
            def _(v):
                ch = P.chain()
                for j in range(NE):
                    s_whi.wait(v, j + 1)
                    s_wk.inc(v.tensor_sub(w_lo[:, j * HW:(j + 1) * HW], wf32[j % 2][:],
                                          w_hi[:, j * HW:(j + 1) * HW]))
                for n, i in enumerate(tiles):
                    if rope:
                        ctab = cos_tab[:, ts(taboff + n)]
                        stab = sin_tab[:, ts(taboff + n)]
                        for fc in range(2):
                            s_ps.wait(v, 2 * n + fc + 1)
                            if n >= 2 and fc == 0:
                                s_tp.wait(v, 2 * (n - 1))
                            b = 4 + fc
                            last = None
                            for hh in range(4):
                                col = fc * 512 + hh * 128
                                kk = PB[b][:, ts(hh)]
                                kro = prj[n % 2][:, col:col + 128]
                                v.tensor_tensor(kro, kk, ctab, ALU.mult)
                                v.tensor_tensor(rtmp[:, col:col + 64],
                                                kk[:, 64:128], stab[:, 0:64], ALU.mult)
                                last = v.tensor_tensor(rtmp[:, col + 64:col + 128],
                                                       kk[:, 0:64], stab[:, 64:128], ALU.mult)
                            ch.cut(v, last)
                            for hh in range(4):
                                col = fc * 512 + hh * 128
                                kro = prj[n % 2][:, col:col + 128]
                                v.tensor_sub(kro[:, 0:64], kro[:, 0:64], rtmp[:, col:col + 64])
                                last = v.tensor_add(kro[:, 64:128], kro[:, 64:128],
                                                    rtmp[:, col + 64:col + 128])
                            s_rp.inc(last)
                        for grp in range(2):
                            s_hi.wait(v, 2 * n + grp + 1)
                            s_ev.inc(v.tensor_sub(ol[n % 2][:, ts(grp, 512)], PB[grp][:],
                                                  oh[n % 2][:, ts(grp, 512)]))
                    else:
                        for fc in range(2):
                            s_hi.wait(v, 2 * n + fc + 1)
                            s_ev.inc(v.tensor_sub(
                                vap2(ol[n % 2], fc),
                                PB[4 + fc][:].rearrange("p (h c) -> p h c", c=128),
                                vap2(oh[n % 2], fc)))

            @blk.tensor
            def _(t):
                s_wk.wait(t, NE)
                for n, i in enumerate(tiles):
                    s_a[n % 2].wait(t, 32 * (n // 2 + 1))
                    for fc in range(2):
                        b = 4 + fc
                        idx = 2 * n + fc
                        if idx >= 2:
                            if rope:
                                s_rp.wait(t, idx - 1)
                            else:
                                s_ev.wait(t, idx - 1)
                        for j in range(NE):
                            ah = a_hi[n % 2][:, ts(j)]
                            al = a_lo[n % 2][:, ts(j)]
                            bh = w_hi[:, j * HW + fc * 512: j * HW + fc * 512 + 512]
                            bl = w_lo[:, j * HW + fc * 512: j * HW + fc * 512 + 512]
                            t.matmul(PB[b][:], ah, bh, start=(j == 0), stop=False)
                            t.matmul(PB[b][:], ah, bl, start=False, stop=False)
                            mm = t.matmul(PB[b][:], al, bh, start=False, stop=(j == NE - 1))
                        s_ps.inc(mm)
                    if rope:
                        s_rp.wait(t, 2 * (n + 1))
                        if n >= 1:
                            s_ev.wait(t, 2 * n)
                        for j in range(8):
                            b = j // 4
                            mm = t.matmul(PB[b][:, ts(j % 4)], prj[n % 2][:, ts(j)], ident[:],
                                          is_transpose=True, start=(j % 4 == 0), stop=(j % 4 == 3))
                            if j % 4 == 3:
                                s_tp.inc(mm)

    if en("B"):
        for fh in range(2):
            proj_phase("wkT", "kfm", kfm_hi, kfm_lo, list(range(NT)), fh, taboff=0)
    if en("C"):
        for fh in range(2):
            proj_phase("wvT", "vtm", vtm_hi, vtm_lo, list(range(NT)), fh)
    if en("D"):
        for fh in range(2):
            proj_phase("wqT", "qfm", qfm_hi, qfm_lo, list(range(8, 16)), fh, taboff=8)

    if en("E"):
        KTS = {0: list(range(12)), 1: list(range(16))}
        with Phase(POOL) as P, ExitStack() as X:
            s_sc, s_msk, s_p, s_ph, s_pl, s_pvk, s_at, s_tp, s_hi, s_ev = (
                P.sem() for _ in range(10))
            NSC_END = []
            _tot = 0
            for _hq in range(2 * H):
                _tot += len(KTS[_hq % 2])
                NSC_END.append(_tot)
            s_kv = [P.sem('s'), P.sem('s')]
            s_sp = [P.sem('g'), P.sem('g')]
            def boundary(qt, kt):
                m = kt - (8 + 4 * qt)
                return m if 0 <= m < 4 else None

            sb = lambda nm, sh, dt: _sbuf(X, nm, sh, dt)
            khb = [sb(f"khb{b}", [128, T], BF16) for b in range(2)]
            klb = [sb(f"klb{b}", [128, T], BF16) for b in range(2)]
            qhb = [sb(f"qhb{b}", [128, TQ], BF16) for b in range(2)]
            qlb = [sb(f"qlb{b}", [128, TQ], BF16) for b in range(2)]
            vhb = [sb(f"vhb{b}", [128, NT * VB], BF16) for b in range(2)]
            vlb = [sb(f"vlb{b}", [128, NT * VB], BF16) for b in range(2)]
            pf = [sb(f"pf{b}", [128, 512], F32) for b in range(2)]
            phb = [sb(f"phb{b}", [128, 512], BF16) for b in range(2)]
            plb = [sb(f"plb{b}", [128, 512], BF16) for b in range(2)]
            dnb = sb("dnb", [128, 8], F32)
            atm = sb("atm", [128, 512], F32)
            aoh = [sb(f"aoh{b}", [128, 512], BF16) for b in range(2)]
            aol = [sb(f"aol{b}", [128, 512], BF16) for b in range(2)]
            blk = X.enter_context(nc.Block())

            @blk.sync
            def _(sync):
                for h in range(H):
                    if h >= 2:
                        s_pvk.wait(sync, NSC_END[2 * h - 3])
                    sk = s_kv[h % 2]
                    sk.inc(sync.dma_start(out=khb[h % 2][:], in_=kfm_hi[ts(h), :]), 16)
                    sk.inc(sync.dma_start(out=klb[h % 2][:], in_=kfm_lo[ts(h), :]), 16)
                    sk.inc(sync.dma_start(out=qhb[h % 2][:], in_=qfm_hi[ts(h), :]), 16)
                    sk.inc(sync.dma_start(out=qlb[h % 2][:], in_=qfm_lo[ts(h), :]), 16)
                    sk.inc(sync.dma_start(
                        out=vhb[h % 2].rearrange("p (j c) -> p j c", c=VB),
                        in_=bass.AP(vtm_hi.tensor, h * VB,
                                    [[H * VB, 128], [128 * H * VB, NT], [1, VB]])), 16)
                    sk.inc(sync.dma_start(
                        out=vlb[h % 2].rearrange("p (j c) -> p j c", c=VB),
                        in_=bass.AP(vtm_lo.tensor, h * VB,
                                    [[H * VB, 128], [128 * H * VB, NT], [1, VB]])), 16)

            @blk.gpsimd
            def _(g):
                for hq in range(2 * H):
                    h, qt = divmod(hq, 2)
                    s_ev.wait(g, hq + 1)
                    s_sp[hq % 2].inc(g.dma_start(out=atf_hi[ts(h), ts(qt, 512)],
                                                 in_=aoh[hq % 2][:]), 16)
                    s_sp[hq % 2].inc(g.dma_start(out=atf_lo[ts(h), ts(qt, 512)],
                                                 in_=aol[hq % 2][:]), 16)
                s_sp[0].wait(g, 32 * H)
                s_sp[1].wait(g, 32 * H)

            @blk.tensor
            def _(t):
                nsc = 0
                for h in range(H):
                    s_kv[h % 2].wait(t, 96 * (h // 2 + 1))
                    for qt in (0, 1):
                        hq = 2 * h + qt
                        kts = KTS[qt]
                        for z, kt in enumerate(kts):
                            nsc += 1
                            sbank = PB[nsc % 2]
                            if nsc > 2:
                                s_p.wait(t, nsc - 2)
                            kh = khb[h % 2][:, ts(kt)]
                            kl = klb[h % 2][:, ts(kt)]
                            qh = qhb[h % 2][:, ts(qt, 512)]
                            ql = qlb[h % 2][:, ts(qt, 512)]
                            t.matmul(sbank[:], kh, qh, start=True, stop=False)
                            t.matmul(sbank[:], kh, ql, start=False, stop=False)
                            t.matmul(sbank[:], kl, qh, start=False, stop=False)
                            mm = t.matmul(sbank[:], ck_bf[0:1, ts(kt)], ones512[:],
                                          start=False, stop=True)
                            s_sc.inc(mm)
                            s_pl.wait(t, nsc)
                            if z == 0:
                                s_at.wait(t, hq)
                            first, last = (z == 0), (z == len(kts) - 1)
                            for sub in range(4):
                                pvb = PB[2 + sub // 2]
                                col = (sub % 2) * 256
                                ph = phb[nsc % 2][:, ts(sub)]
                                pl = plb[nsc % 2][:, ts(sub)]
                                vh = vhb[h % 2][:, kt * VB: kt * VB + 129]
                                vl = vlb[h % 2][:, kt * VB: kt * VB + 129]
                                t.matmul(pvb[:, col:col + 129], ph, vh,
                                         start=(first and sub in (0, 2)), stop=False)
                                t.matmul(pvb[:, col:col + 129], ph, vl,
                                         start=False, stop=False)
                                mm = t.matmul(pvb[:, col:col + 129], pl, vh, start=False,
                                              stop=(last and sub in (1, 3)))
                            s_pvk.inc(mm)
                        s_at.wait(t, hq + 1)
                        if hq >= 1:
                            s_ev.wait(t, hq)
                        for sub in range(4):
                            mm = t.matmul(PB[4][:, ts(sub)], atm[:, ts(sub)], ident[:],
                                          is_transpose=True, start=(sub == 0), stop=(sub == 3))
                        s_tp.inc(mm)

            @blk.vector
            def _(v):
                ch = P.chain()
                nsc = 0
                for h in range(H):
                    for qt in (0, 1):
                        hq = 2 * h + qt
                        kts = KTS[qt]
                        for z, kt in enumerate(kts):
                            nsc += 1
                            sbank = PB[nsc % 2]
                            m = boundary(qt, kt)
                            s_sc.wait(v, nsc)
                            if m is not None:
                                s_msk.inc(v.tensor_tensor(sbank[:], sbank[:],
                                                          amask[:, ts(m, 512)], ALU.add))
                            else:
                                s_msk.inc(v.engine_nop())
                            s_ph.wait(v, nsc)
                            s_pl.inc(v.tensor_sub(plb[nsc % 2][:], pf[nsc % 2][:],
                                                  phb[nsc % 2][:]))
                        s_pvk.wait(v, NSC_END[hq])
                        if hq >= 1:
                            s_tp.wait(v, hq)
                        last = None
                        for sub in range(4):
                            pvb = PB[2 + sub // 2]
                            col = (sub % 2) * 256
                            last = v.reciprocal(dnb[:, sub:sub + 1],
                                                pvb[:, col + 128:col + 129])
                        ch.cut(v, last)
                        for sub in range(4):
                            pvb = PB[2 + sub // 2]
                            col = (sub % 2) * 256
                            last = v.tensor_scalar_mul(atm[:, ts(sub)], pvb[:, col:col + 128],
                                                       dnb[:, sub:sub + 1])
                        s_at.inc(last)
                        s_hi.wait(v, hq + 1)
                        s_ev.inc(v.tensor_sub(aol[hq % 2][:], PB[4][:], aoh[hq % 2][:]))

            @blk.scalar
            def _(sc):
                nsc = 0
                for h in range(H):
                    for qt in (0, 1):
                        hq = 2 * h + qt
                        for z, kt in enumerate(KTS[qt]):
                            nsc += 1
                            sbank = PB[nsc % 2]
                            s_msk.wait(sc, nsc)
                            if nsc > 2:
                                s_pl.wait(sc, nsc - 2)
                                s_pvk.wait(sc, nsc - 2)
                            s_p.inc(sc.activation(pf[nsc % 2][:], sbank[:], ACTF.Exp,
                                                  bias=EXP_BIAS, scale=SCALE))
                            s_p.wait(sc, nsc)
                            s_ph.inc(sc.activation(phb[nsc % 2][:], pf[nsc % 2][:],
                                                   ACTF.Copy))
                        s_tp.wait(sc, hq + 1)
                        if hq >= 2:
                            s_sp[hq % 2].wait(sc, 32 * ((hq - 2) // 2 + 1))
                        s_hi.inc(sc.activation(aoh[hq % 2][:], PB[4][:], ACTF.Copy))

    def oproj_phase(fh):
        with Phase(POOL) as P, ExitStack() as X:
            s_whi, s_wk, s_ps, s_oev = (P.sem() for _ in range(4))
            s_wld = [P.sem('s'), P.sem('s')]
            s_a = [P.sem('s'), P.sem('s')]
            s_sp = [P.sem('g'), P.sem('g')]
            sb = lambda nm, sh, dt: _sbuf(X, nm, sh, dt)
            wf32 = [sb(f"wf32{b}", [128, 1024], F32) for b in range(2)]
            w_hi = sb("w_hi", [128, NE * 1024], BF16)
            w_lo = sb("w_lo", [128, NE * 1024], BF16)
            a_hi = [sb(f"a_hi{b}", [128, E], BF16) for b in range(2)]
            a_lo = [sb(f"a_lo{b}", [128, E], BF16) for b in range(2)]
            ost = [sb(f"ost{b}", [128, 1024], F32) for b in range(2)]
            blk = X.enter_context(nc.Block())

            @blk.sync
            def _(sync):
                for j in range(NE):
                    if j >= 2:
                        s_wk.wait(sync, j - 1)
                    s_wld[j % 2].inc(sync.dma_start(out=wf32[j % 2][:],
                                                    in_=wT["woT"][ts(j), ts(fh, 1024)]), 16)
                for u in range(NTQ):
                    if u >= 2:
                        s_ps.wait(sync, 2 * (u - 1))
                    s_a[u % 2].inc(sync.dma_start(
                        out=a_hi[u % 2].rearrange("p (j c) -> p j c", c=128),
                        in_=fm3d(atf_hi, TQ, u * 128)), 16)
                    s_a[u % 2].inc(sync.dma_start(
                        out=a_lo[u % 2].rearrange("p (j c) -> p j c", c=128),
                        in_=fm3d(atf_lo, TQ, u * 128)), 16)

            @blk.gpsimd
            def _(g):
                for u in range(NTQ):
                    s_oev.wait(g, 2 * (u + 1))
                    s_sp[u % 2].inc(g.dma_start(out=aor[ts(u), ts(fh, 1024)],
                                                in_=ost[u % 2][:]), 16)
                s_sp[0].wait(g, 16 * (NTQ // 2))
                s_sp[1].wait(g, 16 * (NTQ // 2))

            @blk.scalar
            def _(sc):
                for j in range(NE):
                    s_wld[j % 2].wait(sc, 16 * (j // 2 + 1))
                    s_whi.inc(sc.activation(w_hi[:, ts(j, 1024)], wf32[j % 2][:],
                                            ACTF.Copy))
                for u in range(NTQ):
                    for fc in range(2):
                        s_ps.wait(sc, 2 * u + fc + 1)
                        if u >= 2:
                            s_sp[u % 2].wait(sc, 16 * ((u - 2) // 2 + 1))
                        s_oev.inc(sc.activation(ost[u % 2][:, ts(fc, 512)], PB[fc][:],
                                                ACTF.Copy))

            @blk.vector
            def _(v):
                for j in range(NE):
                    s_whi.wait(v, j + 1)
                    s_wk.inc(v.tensor_sub(w_lo[:, ts(j, 1024)], wf32[j % 2][:],
                                          w_hi[:, ts(j, 1024)]))

            @blk.tensor
            def _(t):
                s_wk.wait(t, NE)
                for u in range(NTQ):
                    s_a[u % 2].wait(t, 32 * (u // 2 + 1))
                    for fc in range(2):
                        idx = 2 * u + fc
                        if idx >= 2:
                            s_oev.wait(t, idx - 1)
                        b = PB[fc]
                        for j in range(NE):
                            ah = a_hi[u % 2][:, ts(j)]
                            al = a_lo[u % 2][:, ts(j)]
                            bh = w_hi[:, j * 1024 + fc * 512: j * 1024 + fc * 512 + 512]
                            bl = w_lo[:, j * 1024 + fc * 512: j * 1024 + fc * 512 + 512]
                            t.matmul(b[:], ah, bh, start=(j == 0), stop=False)
                            t.matmul(b[:], ah, bl, start=False, stop=False)
                            mm = t.matmul(b[:], al, bh, start=False, stop=(j == NE - 1))
                        s_ps.inc(mm)

    if en("F"):
        oproj_phase(0)
        oproj_phase(1)

    if en("R"):
        with Phase(POOL) as P, ExitStack() as X:
            s_sq, s_var, s_sd, s_x2 = (P.sem() for _ in range(4))
            s_ar = [P.sem('s'), P.sem('s')]
            s_xq = [P.sem('s'), P.sem('s')]
            s_sp = [P.sem('g'), P.sem('g')]
            sb = lambda nm, sh, dt: _sbuf(X, nm, sh, dt)
            ar = [sb(f"ar{b}", [128, E], F32) for b in range(2)]
            xq = [sb(f"xq{b}", [128, E], F32) for b in range(2)]
            scrap = sb("scrap", [128, E], F32)
            lstat = [sb(f"lstat{b}", [128, 8], F32) for b in range(2)]
            x2t = [sb(f"x2t{b}", [128, E], F32) for b in range(2)]
            blk = X.enter_context(nc.Block())

            @blk.sync
            def _(sync):
                for u in range(NTQ):
                    if u >= 2:
                        s_x2.wait(sync, u - 1)
                    s_ar[u % 2].inc(sync.dma_start(out=ar[u % 2][:], in_=aor[ts(u), :]), 16)
                    s_xq[u % 2].inc(sync.dma_start(out=xq[u % 2][:], in_=xkv[ts(u + 8), :]), 16)

            @blk.gpsimd
            def _(g):
                for u in range(NTQ):
                    s_x2.wait(g, u + 1)
                    s_sp[u % 2].inc(g.dma_start(out=x2sp[ts(u), :], in_=x2t[u % 2][:]), 16)
                s_sp[0].wait(g, 16 * (NTQ // 2))
                s_sp[1].wait(g, 16 * (NTQ // 2))

            @blk.scalar
            def _(sc):
                for u in range(NTQ):
                    st = lstat[u % 2]
                    s_ar[u % 2].wait(sc, 16 * (u // 2 + 1))
                    s_sq.inc(sc.activation(scrap[:], ar[u % 2][:], ACTF.Square,
                                           accum_out=st[:, 1:2]))
                    s_var.wait(sc, u + 1)
                    s_sd.inc(sc.activation(st[:, 5:6], st[:, 4:5], ACTF.Sqrt))

            @blk.vector
            def _(v):
                ch = P.chain()
                for u in range(NTQ):
                    st = lstat[u % 2]
                    s_sq.wait(v, u + 1)
                    s_var.inc(ln_stats_dve(v, st, ar[u % 2][:], ch))
                    s_sd.wait(v, u + 1)
                    ch.cut(v, v.reciprocal(st[:, 6:7], st[:, 5:6]))
                    if u >= 2:
                        s_sp[u % 2].wait(v, 16 * ((u - 2) // 2 + 1))
                    ch.cut(v, v.tensor_scalar(x2t[u % 2][:], ar[u % 2][:], st[:, 3:4],
                                              st[:, 6:7], ALU.subtract, ALU.mult))
                    s_xq[u % 2].wait(v, 16 * (u // 2 + 1))
                    s_x2.inc(v.tensor_add(x2t[u % 2][:], x2t[u % 2][:], xq[u % 2][:]))

    if en("G"):
        ln_fm_phase(x2sp, x2lt_hi, x2lt_lo, NTQ, TQ)

    if en("Z"):
        for tq in range(4):
            with Phase(POOL) as P, ExitStack() as X:
                s_gps, s_sil, s_fu, s_fhi, s_flo = (P.sem() for _ in range(5))
                s_dps, s_ffd, s_ftp, s_fev, s_sq, s_var, s_sd, s_add = (
                    P.sem() for _ in range(8))
                s_x2l, s_x2r = P.sem('s'), P.sem('s')
                s_ost = P.sem('g')
                s_wg = [P.sem('g'), P.sem('g')]
                s_wd = [P.sem('g'), P.sem('g')]
                sb = lambda nm, sh, dt: _sbuf(X, nm, sh, dt)
                xt2h = sb("xt2h", [128, NE * 256], BF16)
                xt2l = sb("xt2l", [128, NE * 256], BF16)
                fush = sb("fush", [128, 64 * 256], BF16)
                fusl = sb("fusl", [128, 64 * 256], BF16)
                wgb = [sb(f"wgb{b}", [128, E], BF16) for b in range(2)]
                wub = [sb(f"wub{b}", [128, E], BF16) for b in range(2)]
                wdb = [sb(f"wdb{b}", [128, E], BF16) for b in range(2)]
                silb = [sb(f"silb{b}", [128, 256], F32) for b in range(2)]
                fub = [sb(f"fub{b}", [128, 256], F32) for b in range(2)]
                ffd = sb("ffd", [128, 16 * 256], F32)
                ftm = sb("ftm", [128, 2 * E], F32)
                x2r = sb("x2r", [128, E], F32)
                scrap = sb("scrapq", [128, E], F32)
                lstat = sb("lstatq", [128, 8], F32)
                otile = sb("otile", [128, E], F32)
                blk = X.enter_context(nc.Block())

                @blk.sync
                def _(sync):
                    s_x2l.inc(sync.dma_start(
                        out=xt2h.rearrange("p (j c) -> p j c", c=256),
                        in_=fm3d(x2lt_hi, TQ, tq * 256, width=256)), 16)
                    s_x2l.inc(sync.dma_start(
                        out=xt2l.rearrange("p (j c) -> p j c", c=256),
                        in_=fm3d(x2lt_lo, TQ, tq * 256, width=256)), 16)
                    for sub in range(2):
                        if sub == 1:
                            s_add.wait(sync, 1)
                        s_x2r.inc(sync.dma_start(out=x2r[:],
                                                 in_=x2sp[ts(2 * tq + sub), :]), 16)

                @blk.gpsimd
                def _(g):
                    for c in range(64):
                        if c >= 2:
                            s_gps.wait(g, c - 1)
                        s_wg[c % 2].inc(g.dma_start(
                            out=wgb[c % 2].rearrange("p (j c2) -> p j c2", c2=128),
                            in_=bass.AP(wg8.tensor, c * 128,
                                        [[FI, 128], [128 * FI, NE], [1, 128]])), 16)
                        s_wg[c % 2].inc(g.dma_start(
                            out=wub[c % 2].rearrange("p (j c2) -> p j c2", c2=128),
                            in_=bass.AP(wu8.tensor, c * 128,
                                        [[FI, 128], [128 * FI, NE], [1, 128]])), 16)
                    for o in range(64):
                        if o >= 2:
                            s_dps.wait(g, o - 1)
                        s_wd[o % 2].inc(g.dma_start(out=wdb[o % 2][:], in_=wd8[ts(o), :]), 16)
                    for sub in range(2):
                        s_add.wait(g, sub + 1)
                        s_ost.inc(g.dma_start(out=out[ts(2 * tq + sub), :],
                                              in_=otile[:]), 16)
                        s_ost.wait(g, 16 * (sub + 1))

                @blk.tensor
                def _(t):
                    s_x2l.wait(t, 32)
                    for c in range(64):
                        s_wg[c % 2].wait(t, 32 * (c // 2 + 1))
                        if c >= 2:
                            s_fu.wait(t, c - 1)
                        b = PB[c % 2]
                        for j in range(NE):
                            wgj = wgb[c % 2][:, ts(j)]
                            wuj = wub[c % 2][:, ts(j)]
                            xh = xt2h[:, j * 256: j * 256 + 256]
                            xl2 = xt2l[:, j * 256: j * 256 + 256]
                            t.matmul(b[:, 0:256], wgj, xh, start=(j == 0), stop=False)
                            t.matmul(b[:, 0:256], wgj, xl2, start=False, stop=False)
                            t.matmul(b[:, 256:512], wuj, xh, start=False, stop=False)
                            mm = t.matmul(b[:, 256:512], wuj, xl2, start=False,
                                          stop=(j == NE - 1))
                        s_gps.inc(mm)
                    for o in range(64):
                        s_wd[o % 2].wait(t, 16 * (o // 2 + 1))
                        if o == 0:
                            s_fu.wait(t, 64)
                        s_flo.wait(t, o + 1)
                        for hc in range(16):
                            b = PB[hc // 2]
                            col = (hc % 2) * 256
                            fh = fush[:, o * 256: o * 256 + 256]
                            fl = fusl[:, o * 256: o * 256 + 256]
                            wdj = wdb[o % 2][:, ts(hc)]
                            t.matmul(b[:, col:col + 256], wdj, fh,
                                     start=(o == 0 and hc % 2 == 0), stop=False)
                            mm = t.matmul(b[:, col:col + 256], wdj, fl, start=False,
                                          stop=(o == 63 and hc % 2 == 1))
                        s_dps.inc(mm)
                    s_ffd.wait(t, 16)
                    for j in range(32):
                        sub, hc = divmod(j, 16)
                        mm = t.matmul(PB[j // 4][:, ts(j % 4)],
                                      ffd[:, hc * 256 + sub * 128: hc * 256 + sub * 128 + 128],
                                      ident[:], is_transpose=True,
                                      start=(j % 4 == 0), stop=(j % 4 == 3))
                        if j % 4 == 3:
                            s_ftp.inc(mm)

                @blk.scalar
                def _(sc):
                    for c in range(64):
                        s_gps.wait(sc, c + 1)
                        if c >= 2:
                            s_fu.wait(sc, c - 1)
                        b = PB[c % 2]
                        s_sil.inc(sc.activation(silb[c % 2][:], b[:, 0:256], ACTF.Silu,
                                                scale=sgate_p[:, c:c + 1]))
                        s_fu.wait(sc, c + 1)
                        s_fhi.inc(sc.activation(fush[:, ts(c, 256)], fub[c % 2][:],
                                                ACTF.Copy))
                    for g8 in range(8):
                        s_ftp.wait(sc, g8 + 1)
                        j0 = 4 * g8
                        sub = j0 // 16
                        s_fev.inc(sc.activation(
                            ftm[:, sub * E + (j0 % 16) * 128: sub * E + (j0 % 16) * 128 + 512],
                            PB[g8][:], ACTF.Copy))
                    for sub in range(2):
                        s_fev.wait(sc, 4 * (sub + 1))
                        s_sq.inc(sc.activation(scrap[:], ftm[:, ts(sub, E)], ACTF.Square,
                                               accum_out=lstat[:, 1:2]))
                        s_var.wait(sc, sub + 1)
                        s_sd.inc(sc.activation(lstat[:, 5:6], lstat[:, 4:5], ACTF.Sqrt))

                @blk.vector
                def _(v):
                    ch = P.chain()
                    for c in range(64):
                        s_sil.wait(v, c + 1)
                        b = PB[c % 2]
                        ch.cut(v, v.tensor_scalar_mul(silb[c % 2][:], silb[c % 2][:],
                                                      sup_p[:, c:c + 1]))
                        if c >= 2:
                            s_fhi.wait(v, c - 1)
                        s_fu.inc(v.tensor_tensor(fub[c % 2][:], silb[c % 2][:],
                                                 b[:, 256:512], ALU.mult))
                        s_fhi.wait(v, c + 1)
                        s_flo.inc(v.tensor_sub(fusl[:, ts(c, 256)], fub[c % 2][:],
                                               fush[:, ts(c, 256)]))
                    s_dps.wait(v, 64)
                    for hc in range(16):
                        b = PB[hc // 2]
                        col = (hc % 2) * 256
                        s_ffd.inc(v.tensor_scalar_mul(ffd[:, ts(hc, 256)],
                                                      b[:, col:col + 256],
                                                      sdown_p[:, hc:hc + 1]))
                    for sub in range(2):
                        s_sq.wait(v, sub + 1)
                        s_var.inc(ln_stats_dve(v, lstat, ftm[:, ts(sub, E)], ch))
                        s_sd.wait(v, sub + 1)
                        ch.cut(v, v.reciprocal(lstat[:, 6:7], lstat[:, 5:6]))
                        if sub == 1:
                            s_ost.wait(v, 16)
                        ch.cut(v, v.tensor_scalar(otile[:], ftm[:, ts(sub, E)],
                                                  lstat[:, 3:4], lstat[:, 6:7],
                                                  ALU.subtract, ALU.mult))
                        s_x2r.wait(v, 16 * (sub + 1))
                        s_add.inc(v.tensor_add(otile[:], otile[:], x2r[:]))

    return nc



def kernel(L0_attn_x, L0_attn_wq, L0_attn_wk, L0_attn_wv, L0_attn_wo,
           L0_ffn_w_gate, L0_ffn_w_up, L0_ffn_w_down,
           L0_ffn_s_gate, L0_ffn_s_up, L0_ffn_s_down):
    B = L0_attn_x.shape[0]
    shared = dict(
        wqT=np.ascontiguousarray(np.asarray(L0_attn_wq).T),
        wkT=np.ascontiguousarray(np.asarray(L0_attn_wk).T),
        wvT=np.ascontiguousarray(np.asarray(L0_attn_wv).T),
        woT=np.ascontiguousarray(np.asarray(L0_attn_wo).T),
        w_gate=np.ascontiguousarray(np.asarray(L0_ffn_w_gate)),
        w_up=np.ascontiguousarray(np.asarray(L0_ffn_w_up)),
        w_down=np.ascontiguousarray(np.asarray(L0_ffn_w_down)),
        s_gate=np.ascontiguousarray(np.asarray(L0_ffn_s_gate).reshape(1, FI)),
        s_up=np.ascontiguousarray(np.asarray(L0_ffn_s_up).reshape(1, FI)),
        s_down=np.ascontiguousarray(np.asarray(L0_ffn_s_down).reshape(1, E)),
    )
    x = np.asarray(L0_attn_x)
    in_maps = []
    for c in range(2 * B):
        p, hhalf = divmod(c, 2)
        seq = x[p]
        if hhalf == 0:
            x_kv = np.concatenate([np.zeros((TQ, E), np.float32), seq[0:TQ]], axis=0)
            pos_v = np.arange(-TQ, TQ, dtype=np.float32)
        else:
            x_kv = seq
            pos_v = np.arange(0, T, dtype=np.float32)
        in_maps.append(dict(shared, x_kv=np.ascontiguousarray(x_kv),
                            pos=np.ascontiguousarray(pos_v.reshape(1, T))))
    nc = build_nc()
    res = run_bass_kernel_spmd(nc, in_maps, core_ids=list(range(2 * B)))
    outp = np.zeros((B, T, E), np.float32)
    for c, r in enumerate(res.results):
        p, hhalf = divmod(c, 2)
        outp[p, hhalf * TQ:(hhalf + 1) * TQ] = r["out"]
    return outp


# version 02
# speedup vs baseline: 1.1612x; 1.1612x over previous
import math
from contextlib import ExitStack
import numpy as np
import concourse.bass as bass
import concourse.mybir as mybir
from concourse.bass_utils import run_bass_kernel_spmd

F32 = mybir.dt.float32
BF16 = mybir.dt.bfloat16
I8 = mybir.dt.int8
I32 = mybir.dt.int32
AX = mybir.AxisListType.X
ALU = mybir.AluOpType
ACTF = mybir.ActivationFunctionType

T, TQ, E, H, DH, FI = 2048, 1024, 2048, 16, 128, 8192
NE, NT, NTQ = E // 128, T // 128, TQ // 128
VB = 130
SCALE = 1.0 / math.sqrt(DH)
EXP_BIAS = -4.0
TWO_PI, INV_2PI = 2 * math.pi, 1.0 / (2 * math.pi)
PI_CLAMP = 3.141592
LN1E6 = math.log(1.0e6)
CKMUL = float(2 ** 21)
EPS = 1e-5
NOPS = 3


def ts(i, n=128):
    return slice(i * n, (i + 1) * n)


def nops(eng, n=NOPS):
    for _ in range(n):
        eng.engine_nop()


class SemPool:

    SIZES = {"c": 17, "s": 6, "g": 5}

    def __init__(self, nc):
        n = sum(self.SIZES.values())
        self.sems = [nc.alloc_semaphore(f"sp{i}") for i in range(n)]
        self.base = [0] * n
        self.free = {}
        k = 0
        for kind, cnt in self.SIZES.items():
            self.free[kind] = list(range(k, k + cnt))
            k += cnt

    def alloc(self, kind="c"):
        i = self.free[kind].pop()
        return SemRef(self, i, kind)

    def release(self, ref):
        self.base[ref.i] += ref.count
        self.free[ref.kind].append(ref.i)


class SemRef:
    def __init__(self, pool, i, kind="c"):
        self.pool, self.i, self.kind = pool, i, kind
        self.h = pool.sems[i]
        self.b = pool.base[i]
        self.count = 0

    def inc(self, inst, k=1):
        inst.then_inc(self.h, k)
        self.count += k
        return inst

    def wait(self, eng, k):
        eng.wait_ge(self.h, self.b + k)


class Phase:

    def __init__(self, pool):
        self.pool = pool
        self.refs = []

    def sem(self, kind="c"):
        r = self.pool.alloc(kind)
        self.refs.append(r)
        return r

    def chain(self):
        return Chain(self.sem())

    def __enter__(self):
        return self

    def __exit__(self, *a):
        for r in self.refs:
            self.pool.release(r)
        return False


class Chain:

    def __init__(self, sref):
        self.s = sref
        self.n = 0

    def cut(self, eng, inst):
        self.n += 1
        inst.then_inc(self.s.h, 1)
        self.s.count += 1
        eng.wait_ge(self.s.h, self.s.b + self.n)
        return inst


def build_nc(expose=(), upto="Z"):
    nc = bass.Bass("TRN2", target_bir_lowering=False)
    PH = "0ABCDEFRGZ"
    lim = PH.index(upto)

    def en(p):
        return PH.index(p) <= lim

    def dram(name, shape, dtype):
        kind = "ExternalOutput" if name in expose else "Internal"
        return nc.dram_tensor(name, shape, dtype, kind=kind).ap()

    _uid = [0]

    def _sbuf(X, nm, sh, dt):
        _uid[0] += 1
        return X.enter_context(nc.sbuf_tensor(f"{nm}_u{_uid[0]}", sh, dt))

    def fm3d(ap2d, rowstride, tcol, width=128, nchunk=NE, rowoff=0):
        return bass.AP(ap2d.tensor, rowoff * rowstride + tcol,
                       [[rowstride, 128], [128 * rowstride, nchunk], [1, width]])

    xkv = nc.dram_tensor("x_kv", [T, E], F32, kind="ExternalInput").ap()
    pos = nc.dram_tensor("pos", [1, T], F32, kind="ExternalInput").ap()
    wT = {}
    for nm in ("wqT", "wkT", "wvT", "woT"):
        wT[nm] = nc.dram_tensor(nm, [E, E], F32, kind="ExternalInput").ap()
    wg8 = nc.dram_tensor("w_gate", [E, FI], I8, kind="ExternalInput").ap()
    wu8 = nc.dram_tensor("w_up", [E, FI], I8, kind="ExternalInput").ap()
    wd8 = nc.dram_tensor("w_down", [FI, E], I8, kind="ExternalInput").ap()
    sgin = nc.dram_tensor("s_gate", [1, FI], F32, kind="ExternalInput").ap()
    suin = nc.dram_tensor("s_up", [1, FI], F32, kind="ExternalInput").ap()
    sdin = nc.dram_tensor("s_down", [1, E], F32, kind="ExternalInput").ap()
    out = nc.dram_tensor("out", [TQ, E], F32, kind="ExternalOutput").ap()

    xlt_hi, xlt_lo = dram("xlt_hi", [E, T], BF16), dram("xlt_lo", [E, T], BF16)
    kfm_hi, kfm_lo = dram("kfm_hi", [E, T], BF16), dram("kfm_lo", [E, T], BF16)
    vtm_hi, vtm_lo = dram("vtm_hi", [T, H * VB], BF16), dram("vtm_lo", [T, H * VB], BF16)
    qfm_hi, qfm_lo = dram("qfm_hi", [E, TQ], BF16), dram("qfm_lo", [E, TQ], BF16)
    atf_hi, atf_lo = dram("atf_hi", [E, TQ], BF16), dram("atf_lo", [E, TQ], BF16)
    aor = dram("aor", [TQ, E], F32)
    x2sp = dram("x2sp", [TQ, E], F32)
    x2lt_hi, x2lt_lo = dram("x2lt_hi", [E, TQ], BF16), dram("x2lt_lo", [E, TQ], BF16)

    ident = nc.alloc_sbuf_tensor("ident", [128, 128], F32)
    cpool = nc.alloc_sbuf_tensor("cpool", [128, 2], F32)
    nc.const_aps.register(F32, EXP_BIAS, cpool[:, 0:1])
    cos_tab = nc.alloc_sbuf_tensor("cos_tab", [128, NT * 128], F32)
    sin_tab = nc.alloc_sbuf_tensor("sin_tab", [128, NT * 128], F32)
    amask = nc.alloc_sbuf_tensor("amask", [128, 4 * 512], F32)
    ckcol = nc.alloc_sbuf_tensor("ckcol", [128, NT], F32)
    postile = nc.alloc_sbuf_tensor("postile", [1, T], F32)
    inv_row = nc.alloc_sbuf_tensor("inv_row", [1, 64], F32)
    sgate_p = nc.alloc_sbuf_tensor("sgate_p", [128, 64], F32)
    sup_p = nc.alloc_sbuf_tensor("sup_p", [128, 64], F32)
    sdown_p = nc.alloc_sbuf_tensor("sdown_p", [128, 16], F32)
    PB = [nc.alloc_psum_tensor(f"pb{i}", [128, 512], F32) for i in range(8)]

    POOL = SemPool(nc)

    def ln_stats_dve(v, st, xin, ch):
        ch.cut(v, v.reduce_sum(st[:, 0:1], xin, axis=AX))
        v.tensor_scalar_mul(st[:, 3:4], st[:, 0:1], 1.0 / E)
        ch.cut(v, v.tensor_scalar_mul(st[:, 2:3], st[:, 1:2], 1.0 / E))
        ch.cut(v, v.tensor_tensor(st[:, 4:5], st[:, 3:4], st[:, 3:4], ALU.mult))
        ch.cut(v, v.tensor_sub(st[:, 4:5], st[:, 2:3], st[:, 4:5]))
        return v.tensor_scalar_add(st[:, 4:5], st[:, 4:5], EPS)

    with Phase(POOL) as P, ExitStack() as X:
        s_gp, s_inv, s_ang, s_red, s_cs = (P.sem() for _ in range(5))
        s_ld, s_scl = P.sem('s'), P.sem('s')
        sb = lambda nm, sh, dt: _sbuf(X, nm, sh, dt)
        jrow = sb("jrow", [1, 64], F32)
        z512 = sb("z512", [128, 512], F32)
        rsin = sb("rsin", [128, 1024], F32)
        rcos = sb("rcos", [128, 1024], F32)
        tmpang = sb("tmpang", [128, 1024], F32)
        kint0 = sb("kint0", [128, 1024], I32)
        blk = X.enter_context(nc.Block())

        @blk.sync
        def _(sync):
            s_ld.inc(sync.dma_start(out=postile[:], in_=pos), 16)
            s_ld.inc(sync.dma_start(out=ckcol[:],
                                    in_=bass.AP(pos.tensor, 0, [[1, 128], [128, NT]]),
                                    allow_slow_non_contiguous=True), 16)
            s_scl.inc(sync.dma_start(out=sgate_p[:],
                      in_=sgin.rearrange("one (c p) -> (one p) c", p=128),
                      allow_slow_non_contiguous=True), 16)
            s_scl.inc(sync.dma_start(out=sup_p[:],
                      in_=suin.rearrange("one (c p) -> (one p) c", p=128),
                      allow_slow_non_contiguous=True), 16)
            s_scl.inc(sync.dma_start(out=sdown_p[:],
                      in_=sdin.rearrange("one (c p) -> (one p) c", p=128),
                      allow_slow_non_contiguous=True), 16)
            sync.wait_ge(s_scl.h, s_scl.b + 48)

        @blk.gpsimd
        def _(g):
            g.memset(cpool[:, 0:1], EXP_BIAS)
            g.memset(cpool[:, 1:2], 0.0)
            g.memset(ident[:], 1.0)
            g.memset(z512[:], 0.0)
            g.iota(jrow[:], [[1, 64]], channel_multiplier=0,
                   allow_small_or_imprecise_dtypes=True).then_inc(s_gp.h, 1)
            g.wait_ge(s_gp.h, s_gp.b + 1)
            g.affine_select(ident[:], ident[:], [[-1, 128]], ALU.is_equal, 0.0,
                            channel_multiplier=1)
            for m, base in enumerate((0, 128, 256, 384)):
                g.affine_select(amask[:, ts(m, 512)], z512[:], [[1, 512]],
                                ALU.is_ge, -1.0e9, base=-base, channel_multiplier=-1)
            g.engine_nop().then_inc(s_gp.h, 1)
        s_gp.count += 2

        @blk.scalar
        def _(sc):
            s_gp.wait(sc, 2)
            s_inv.inc(sc.activation(inv_row[:], jrow[:], ACTF.Exp, scale=-LN1E6 / 64.0))
            for b in range(2):
                s_red.wait(sc, b + 1)
                for half in range(2):
                    outap = bass.AP(sin_tab, b * 8 * 128 + half * 64,
                                    [[NT * 128, 128], [128, 8], [1, 64]])
                    sc.activation(outap,
                                  rsin[:, ts(b, 512)].rearrange("p (i c) -> p i c", c=64),
                                  ACTF.Sin)
                s_cs.inc(sc.activation(rcos[:, ts(b, 512)], rsin[:, ts(b, 512)],
                                       ACTF.Sin, scale=0.5))

        @blk.vector
        def _(v):
            ch = P.chain()
            s_ld.wait(v, 32)
            ch.cut(v, v.tensor_scalar_min(ckcol[:], ckcol[:], 0.0))
            ch.cut(v, v.tensor_scalar(ckcol[:], ckcol[:], CKMUL * SCALE, EXP_BIAS,
                                      ALU.mult, ALU.add))
            for b in range(2):
                s_ang.wait(v, b + 1)
                vb, cb = rsin[:, ts(b, 512)], rcos[:, ts(b, 512)]
                tb, kb = tmpang[:, ts(b, 512)], kint0[:, ts(b, 512)]
                pbv = PB[b][:]
                ch.cut(v, v.tensor_scalar_mul(tb, pbv, INV_2PI))
                ch.cut(v, v.tensor_copy(kb, tb))
                ch.cut(v, v.tensor_copy(tb, kb))
                ch.cut(v, v.scalar_tensor_tensor(vb, tb, -TWO_PI, pbv, ALU.mult, ALU.add))
                ch.cut(v, v.tensor_single_scalar(tb, vb, math.pi, ALU.is_gt))
                ch.cut(v, v.scalar_tensor_tensor(vb, tb, -TWO_PI, vb, ALU.mult, ALU.add))
                ch.cut(v, v.tensor_single_scalar(tb, vb, -math.pi, ALU.is_lt))
                ch.cut(v, v.scalar_tensor_tensor(vb, tb, TWO_PI, vb, ALU.mult, ALU.add))
                ch.cut(v, v.tensor_scalar_min(vb, vb, PI_CLAMP))
                s_red.inc(v.tensor_scalar_max(vb, vb, -PI_CLAMP))
            for b in range(2):
                s_cs.wait(v, b + 1)
                cb = rcos[:, ts(b, 512)]
                tb = tmpang[:, ts(b, 512)]
                ch.cut(v, v.tensor_tensor(tb, cb, cb, ALU.mult))
                for half in range(2):
                    outap = bass.AP(cos_tab, b * 8 * 128 + half * 64,
                                    [[NT * 128, 128], [128, 8], [1, 64]])
                    inap = bass.AP(tmpang, b * 512,
                                   [[1024, 128], [64, 8], [1, 64]])
                    v.tensor_scalar(outap, inap, -2.0, 1.0, ALU.mult, ALU.add)

        @blk.tensor
        def _(t):
            s_inv.wait(t, 1)
            s_ld.wait(t, 32)
            for b in range(2):
                for i in range(8):
                    tile = b * 8 + i
                    mm = t.matmul(PB[b][:, i * 64:(i + 1) * 64],
                                  postile[0:1, ts(tile)], inv_row[:],
                                  start=(i == 0), stop=(i == 7))
                s_ang.inc(mm)

    def ln_fm_phase(src, dst_hi, dst_lo, ntiles, dststride):
        with Phase(POOL) as P, ExitStack() as X:
            s_sq, s_var, s_sd, s_xl, s_tp, s_hi, s_ev = (P.sem() for _ in range(7))
            s_x = [P.sem('s'), P.sem('s')]
            s_sp = [P.sem('g'), P.sem('g')]
            sb = lambda nm, sh, dt: _sbuf(X, nm, sh, dt)
            xrow = [sb(f"xrow{b}", [128, E], F32) for b in range(2)]
            scrap = sb("scrap", [128, E], F32)
            lstat = [sb(f"lstat{b}", [128, 8], F32) for b in range(2)]
            xl = [sb(f"xl{b}", [128, E], F32) for b in range(2)]
            th = [sb(f"th{b}", [128, E], BF16) for b in range(2)]
            tl = [sb(f"tl{b}", [128, E], BF16) for b in range(2)]
            blk = X.enter_context(nc.Block())

            @blk.sync
            def _(sync):
                for i in range(ntiles):
                    if i >= 2:
                        s_xl.wait(sync, i - 1)
                    s_x[i % 2].inc(sync.dma_start(out=xrow[i % 2][:], in_=src[ts(i), :]), 16)

            @blk.gpsimd
            def _(g):
                for i in range(ntiles):
                    s_ev.wait(g, 4 * (i + 1))
                    s_sp[i % 2].inc(g.dma_start(
                        out=fm3d(dst_hi, dststride, i * 128),
                        in_=th[i % 2].rearrange("p (j c) -> p j c", c=128)), 16)
                    s_sp[i % 2].inc(g.dma_start(
                        out=fm3d(dst_lo, dststride, i * 128),
                        in_=tl[i % 2].rearrange("p (j c) -> p j c", c=128)), 16)
                s_sp[0].wait(g, 32 * ((ntiles + 1) // 2))
                s_sp[1].wait(g, 32 * (ntiles // 2))

            @blk.scalar
            def _(sc):
                for i in range(ntiles):
                    st = lstat[i % 2]
                    s_x[i % 2].wait(sc, 16 * (i // 2 + 1))
                    s_sq.inc(sc.activation(scrap[:], xrow[i % 2][:], ACTF.Square,
                                           accum_out=st[:, 1:2]))
                    s_var.wait(sc, i + 1)
                    s_sd.inc(sc.activation(st[:, 5:6], st[:, 4:5], ACTF.Sqrt))
                    for b in range(4):
                        s_tp.wait(sc, 4 * i + b + 1)
                        if i >= 2:
                            s_sp[i % 2].wait(sc, 32 * ((i - 2) // 2 + 1))
                        s_hi.inc(sc.activation(th[i % 2][:, ts(b, 512)], PB[b][:],
                                               ACTF.Copy))

            @blk.vector
            def _(v):
                ch = P.chain()
                for i in range(ntiles):
                    st = lstat[i % 2]
                    s_sq.wait(v, i + 1)
                    s_var.inc(ln_stats_dve(v, st, xrow[i % 2][:], ch))
                    s_sd.wait(v, i + 1)
                    ch.cut(v, v.reciprocal(st[:, 6:7], st[:, 5:6]))
                    if i >= 2:
                        s_tp.wait(v, 4 * (i - 1))
                    s_xl.inc(v.tensor_scalar(xl[i % 2][:], xrow[i % 2][:], st[:, 3:4],
                                             st[:, 6:7], ALU.subtract, ALU.mult))
                    for b in range(4):
                        s_hi.wait(v, 4 * i + b + 1)
                        s_ev.inc(v.tensor_sub(tl[i % 2][:, ts(b, 512)], PB[b][:],
                                              th[i % 2][:, ts(b, 512)]))

            @blk.tensor
            def _(t):
                for i in range(ntiles):
                    s_xl.wait(t, i + 1)
                    if i >= 1:
                        s_ev.wait(t, 4 * i)
                    for j in range(NE):
                        b = j // 4
                        mm = t.matmul(PB[b][:, ts(j % 4)], xl[i % 2][:, ts(j)], ident[:],
                                      is_transpose=True, start=(j % 4 == 0), stop=(j % 4 == 3))
                        if j % 4 == 3:
                            s_tp.inc(mm)

    if en("A"):
        ln_fm_phase(xkv, xlt_hi, xlt_lo, NT, T)

    def proj_phase(wsrc, mode, dst_hi, dst_lo, tiles, fh, taboff=0):
        rope = mode in ("kfm", "qfm")
        dststride = T if mode == "kfm" else TQ
        HW = 1024
        NH = 8 if mode != "vtm" else 8
        ohw = 8 * VB if mode == "vtm" else HW
        with Phase(POOL) as P, ExitStack() as X:
            s_whi, s_wk, s_ps, s_rp, s_tp, s_hi, s_ev, s_vm = (P.sem() for _ in range(8))
            s_wld = [P.sem('s'), P.sem('s')]
            s_a = [P.sem('s'), P.sem('s')]
            s_sp = [P.sem('g'), P.sem('g')]
            sb = lambda nm, sh, dt: _sbuf(X, nm, sh, dt)
            wf32 = [sb(f"wf32{b}", [128, HW], F32) for b in range(2)]
            w_hi = sb("w_hi", [128, NE * HW], BF16)
            w_lo = sb("w_lo", [128, NE * HW], BF16)
            a_hi = [sb(f"a_hi{b}", [128, E], BF16) for b in range(2)]
            a_lo = [sb(f"a_lo{b}", [128, E], BF16) for b in range(2)]
            prj = [sb(f"prj{b}", [128, HW], F32) for b in range(2)]
            rtmp = sb("rtmp", [128, HW], F32)
            oh = [sb(f"oh{b}", [128, ohw], BF16) for b in range(2)]
            ol = [sb(f"ol{b}", [128, ohw], BF16) for b in range(2)]
            blk = X.enter_context(nc.Block())

            def vap(buf, coloff, cnt, nh=8):
                return bass.AP(buf, coloff, [[ohw, 128], [VB, nh], [1, cnt]])

            def vap2(buf, fc):
                return bass.AP(buf, (fc * 4) * VB, [[ohw, 128], [VB, 4], [1, 128]])

            @blk.sync
            def _(sync):
                for j in range(NE):
                    if j >= 2:
                        s_wk.wait(sync, j - 1)
                    s_wld[j % 2].inc(sync.dma_start(out=wf32[j % 2][:],
                                                    in_=wT[wsrc][ts(j), ts(fh, 1024)]), 16)
                for n, i in enumerate(tiles):
                    if n >= 2:
                        s_ps.wait(sync, 2 * (n - 1))
                    s_a[n % 2].inc(sync.dma_start(
                        out=a_hi[n % 2].rearrange("p (j c) -> p j c", c=128),
                        in_=fm3d(xlt_hi, T, i * 128)), 16)
                    s_a[n % 2].inc(sync.dma_start(
                        out=a_lo[n % 2].rearrange("p (j c) -> p j c", c=128),
                        in_=fm3d(xlt_lo, T, i * 128)), 16)

            @blk.gpsimd
            def _(g):
                if mode == "vtm":
                    for buf in range(2):
                        g.memset(vap(oh[buf], 128, 1), 1.0)
                        g.memset(vap(oh[buf], 129, 1), 0.0)
                        lastms = g.memset(vap(ol[buf], 128, 2), 0.0)
                        s_vm.inc(lastms)
                    g.wait_ge(s_vm.h, s_vm.b + 2)
                for n, i in enumerate(tiles):
                    s_ev.wait(g, 2 * (n + 1))
                    if mode == "vtm":
                        s_sp[n % 2].inc(g.dma_start(out=dst_hi[ts(n), ts(fh, 8 * VB)],
                                                    in_=oh[n % 2][:]), 16)
                        s_sp[n % 2].inc(g.dma_start(out=dst_lo[ts(n), ts(fh, 8 * VB)],
                                                    in_=ol[n % 2][:]), 16)
                    else:
                        tc = (i * 128) if mode == "kfm" else ((i - 8) * 128)
                        s_sp[n % 2].inc(g.dma_start(
                            out=fm3d(dst_hi, dststride, tc, nchunk=8, rowoff=fh * 1024),
                            in_=oh[n % 2].rearrange("p (j c) -> p j c", c=128)), 16)
                        s_sp[n % 2].inc(g.dma_start(
                            out=fm3d(dst_lo, dststride, tc, nchunk=8, rowoff=fh * 1024),
                            in_=ol[n % 2].rearrange("p (j c) -> p j c", c=128)), 16)
                s_sp[0].wait(g, 32 * ((len(tiles) + 1) // 2))
                s_sp[1].wait(g, 32 * (len(tiles) // 2))

            @blk.scalar
            def _(sc):
                for j in range(NE):
                    s_wld[j % 2].wait(sc, 16 * (j // 2 + 1))
                    s_whi.inc(sc.activation(w_hi[:, j * HW:(j + 1) * HW], wf32[j % 2][:],
                                            ACTF.Copy))
                for n, i in enumerate(tiles):
                    if rope:
                        for grp in range(2):
                            s_tp.wait(sc, 2 * n + grp + 1)
                            if n >= 2:
                                s_sp[n % 2].wait(sc, 32 * ((n - 2) // 2 + 1))
                            s_hi.inc(sc.activation(oh[n % 2][:, ts(grp, 512)], PB[grp][:],
                                                   ACTF.Copy))
                    else:
                        for fc in range(2):
                            s_ps.wait(sc, 2 * n + fc + 1)
                            if n >= 2:
                                s_sp[n % 2].wait(sc, 32 * ((n - 2) // 2 + 1))
                            s_hi.inc(sc.activation(
                                vap2(oh[n % 2], fc),
                                PB[4 + fc][:].rearrange("p (h c) -> p h c", c=128),
                                ACTF.Copy))

            @blk.vector
            def _(v):
                ch = P.chain()
                for j in range(NE):
                    s_whi.wait(v, j + 1)
                    s_wk.inc(v.tensor_sub(w_lo[:, j * HW:(j + 1) * HW], wf32[j % 2][:],
                                          w_hi[:, j * HW:(j + 1) * HW]))
                for n, i in enumerate(tiles):
                    if rope:
                        ctab = cos_tab[:, ts(taboff + n)]
                        stab = sin_tab[:, ts(taboff + n)]
                        for fc in range(2):
                            s_ps.wait(v, 2 * n + fc + 1)
                            if n >= 2 and fc == 0:
                                s_tp.wait(v, 2 * (n - 1))
                            b = 4 + fc
                            last = None
                            for hh in range(4):
                                col = fc * 512 + hh * 128
                                kk = PB[b][:, ts(hh)]
                                kro = prj[n % 2][:, col:col + 128]
                                v.tensor_tensor(kro, kk, ctab, ALU.mult)
                                v.tensor_tensor(rtmp[:, col:col + 64],
                                                kk[:, 64:128], stab[:, 0:64], ALU.mult)
                                last = v.tensor_tensor(rtmp[:, col + 64:col + 128],
                                                       kk[:, 0:64], stab[:, 64:128], ALU.mult)
                            ch.cut(v, last)
                            for hh in range(4):
                                col = fc * 512 + hh * 128
                                kro = prj[n % 2][:, col:col + 128]
                                v.tensor_sub(kro[:, 0:64], kro[:, 0:64], rtmp[:, col:col + 64])
                                last = v.tensor_add(kro[:, 64:128], kro[:, 64:128],
                                                    rtmp[:, col + 64:col + 128])
                            s_rp.inc(last)
                        for grp in range(2):
                            s_hi.wait(v, 2 * n + grp + 1)
                            s_ev.inc(v.tensor_sub(ol[n % 2][:, ts(grp, 512)], PB[grp][:],
                                                  oh[n % 2][:, ts(grp, 512)]))
                    else:
                        for fc in range(2):
                            s_hi.wait(v, 2 * n + fc + 1)
                            s_ev.inc(v.tensor_sub(
                                vap2(ol[n % 2], fc),
                                PB[4 + fc][:].rearrange("p (h c) -> p h c", c=128),
                                vap2(oh[n % 2], fc)))

            @blk.tensor
            def _(t):
                s_wk.wait(t, NE)
                for n, i in enumerate(tiles):
                    s_a[n % 2].wait(t, 32 * (n // 2 + 1))
                    for fc in range(2):
                        b = 4 + fc
                        idx = 2 * n + fc
                        if idx >= 2:
                            if rope:
                                s_rp.wait(t, idx - 1)
                            else:
                                s_ev.wait(t, idx - 1)
                        for j in range(NE):
                            ah = a_hi[n % 2][:, ts(j)]
                            al = a_lo[n % 2][:, ts(j)]
                            bh = w_hi[:, j * HW + fc * 512: j * HW + fc * 512 + 512]
                            bl = w_lo[:, j * HW + fc * 512: j * HW + fc * 512 + 512]
                            t.matmul(PB[b][:], ah, bh, start=(j == 0), stop=False)
                            t.matmul(PB[b][:], ah, bl, start=False, stop=False)
                            mm = t.matmul(PB[b][:], al, bh, start=False, stop=(j == NE - 1))
                        s_ps.inc(mm)
                    if rope:
                        s_rp.wait(t, 2 * (n + 1))
                        if n >= 1:
                            s_ev.wait(t, 2 * n)
                        for j in range(8):
                            b = j // 4
                            mm = t.matmul(PB[b][:, ts(j % 4)], prj[n % 2][:, ts(j)], ident[:],
                                          is_transpose=True, start=(j % 4 == 0), stop=(j % 4 == 3))
                            if j % 4 == 3:
                                s_tp.inc(mm)

    if en("B"):
        for fh in range(2):
            proj_phase("wkT", "kfm", kfm_hi, kfm_lo, list(range(NT)), fh, taboff=0)
    if en("C"):
        for fh in range(2):
            proj_phase("wvT", "vtm", vtm_hi, vtm_lo, list(range(NT)), fh)
    if en("D"):
        for fh in range(2):
            proj_phase("wqT", "qfm", qfm_hi, qfm_lo, list(range(8, 16)), fh, taboff=8)

    if en("E"):
        KTS = {0: list(range(12)), 1: list(range(16))}
        with Phase(POOL) as P, ExitStack() as X:
            s_sc, s_msk, s_p, s_ph, s_pl, s_pvk, s_at, s_tp, s_hi, s_ev = (
                P.sem() for _ in range(10))
            NSC_END = []
            _tot = 0
            for _hq in range(2 * H):
                _tot += len(KTS[_hq % 2])
                NSC_END.append(_tot)
            s_kv = [P.sem('s'), P.sem('s')]
            s_sp = [P.sem('g'), P.sem('g')]
            def boundary(qt, kt):
                m = kt - (8 + 4 * qt)
                return m if 0 <= m < 4 else None

            sb = lambda nm, sh, dt: _sbuf(X, nm, sh, dt)
            khb = [sb(f"khb{b}", [128, T], BF16) for b in range(2)]
            klb = [sb(f"klb{b}", [128, T], BF16) for b in range(2)]
            qhb = [sb(f"qhb{b}", [128, TQ], BF16) for b in range(2)]
            qlb = [sb(f"qlb{b}", [128, TQ], BF16) for b in range(2)]
            vhb = [sb(f"vhb{b}", [128, NT * VB], BF16) for b in range(2)]
            vlb = [sb(f"vlb{b}", [128, NT * VB], BF16) for b in range(2)]
            pf = [sb(f"pf{b}", [128, 512], F32) for b in range(2)]
            phb = [sb(f"phb{b}", [128, 512], BF16) for b in range(2)]
            plb = [sb(f"plb{b}", [128, 512], BF16) for b in range(2)]
            dnb = sb("dnb", [128, 8], F32)
            atm = sb("atm", [128, 512], F32)
            aoh = [sb(f"aoh{b}", [128, 512], BF16) for b in range(2)]
            aol = [sb(f"aol{b}", [128, 512], BF16) for b in range(2)]
            blk = X.enter_context(nc.Block())

            @blk.sync
            def _(sync):
                for h in range(H):
                    if h >= 2:
                        s_pvk.wait(sync, NSC_END[2 * h - 3])
                    sk = s_kv[h % 2]
                    sk.inc(sync.dma_start(out=khb[h % 2][:], in_=kfm_hi[ts(h), :]), 16)
                    sk.inc(sync.dma_start(out=klb[h % 2][:], in_=kfm_lo[ts(h), :]), 16)
                    sk.inc(sync.dma_start(out=qhb[h % 2][:], in_=qfm_hi[ts(h), :]), 16)
                    sk.inc(sync.dma_start(out=qlb[h % 2][:], in_=qfm_lo[ts(h), :]), 16)
                    sk.inc(sync.dma_start(
                        out=vhb[h % 2].rearrange("p (j c) -> p j c", c=VB),
                        in_=bass.AP(vtm_hi.tensor, h * VB,
                                    [[H * VB, 128], [128 * H * VB, NT], [1, VB]])), 16)
                    sk.inc(sync.dma_start(
                        out=vlb[h % 2].rearrange("p (j c) -> p j c", c=VB),
                        in_=bass.AP(vtm_lo.tensor, h * VB,
                                    [[H * VB, 128], [128 * H * VB, NT], [1, VB]])), 16)

            @blk.gpsimd
            def _(g):
                for hq in range(2 * H):
                    h, qt = divmod(hq, 2)
                    s_ev.wait(g, hq + 1)
                    s_sp[hq % 2].inc(g.dma_start(out=atf_hi[ts(h), ts(qt, 512)],
                                                 in_=aoh[hq % 2][:]), 16)
                    s_sp[hq % 2].inc(g.dma_start(out=atf_lo[ts(h), ts(qt, 512)],
                                                 in_=aol[hq % 2][:]), 16)
                s_sp[0].wait(g, 32 * H)
                s_sp[1].wait(g, 32 * H)

            @blk.tensor
            def _(t):
                SCH = []
                for h in range(H):
                    for qt in (0, 1):
                        for z, kt in enumerate(KTS[qt]):
                            SCH.append((h, qt, z, kt))
                N = len(SCH)

                def emit_qk(i):
                    h, qt, z, kt = SCH[i]
                    if z == 0 and qt == 0:
                        s_kv[h % 2].wait(t, 96 * (h // 2 + 1))
                    nsc = i + 1
                    sbank = PB[nsc % 2]
                    if nsc > 2:
                        s_p.wait(t, nsc - 2)
                    kh = khb[h % 2][:, ts(kt)]
                    kl = klb[h % 2][:, ts(kt)]
                    qh = qhb[h % 2][:, ts(qt, 512)]
                    ql = qlb[h % 2][:, ts(qt, 512)]
                    t.matmul(sbank[:], kh, qh, start=True, stop=False)
                    t.matmul(sbank[:], kh, ql, start=False, stop=False)
                    mm = t.matmul(sbank[:], kl, qh, start=False, stop=True)
                    s_sc.inc(mm)

                def emit_pv(i):
                    h, qt, z, kt = SCH[i]
                    hq = 2 * h + qt
                    nsc = i + 1
                    kts = KTS[qt]
                    s_pl.wait(t, nsc)
                    if z == 0:
                        s_at.wait(t, hq)
                    first, last = (z == 0), (z == len(kts) - 1)
                    for sub in range(4):
                        pvb = PB[2 + sub // 2]
                        col = (sub % 2) * 256
                        ph = phb[nsc % 2][:, ts(sub)]
                        pl = plb[nsc % 2][:, ts(sub)]
                        vh = vhb[h % 2][:, kt * VB: kt * VB + 129]
                        vl = vlb[h % 2][:, kt * VB: kt * VB + 129]
                        t.matmul(pvb[:, col:col + 129], ph, vh,
                                 start=(first and sub in (0, 2)), stop=False)
                        t.matmul(pvb[:, col:col + 129], ph, vl,
                                 start=False, stop=False)
                        mm = t.matmul(pvb[:, col:col + 129], pl, vh, start=False,
                                      stop=(last and sub in (1, 3)))
                    s_pvk.inc(mm)

                def emit_tp(hq):
                    s_at.wait(t, hq + 1)
                    if hq >= 1:
                        s_ev.wait(t, hq)
                    for sub in range(4):
                        mm = t.matmul(PB[4][:, ts(sub)], atm[:, ts(sub)], ident[:],
                                      is_transpose=True, start=(sub == 0), stop=(sub == 3))
                    s_tp.inc(mm)

                for i in range(N):
                    emit_qk(i)
                    if i >= 1:
                        h, qt, z, kt = SCH[i - 1]
                        if z == 0 and (2 * h + qt) >= 1:
                            emit_tp(2 * h + qt - 1)
                        emit_pv(i - 1)
                emit_pv(N - 1)
                emit_tp(2 * H - 1)

            @blk.vector
            def _(v):
                ch = P.chain()
                nsc = 0
                for h in range(H):
                    for qt in (0, 1):
                        hq = 2 * h + qt
                        kts = KTS[qt]
                        for z, kt in enumerate(kts):
                            nsc += 1
                            sbank = PB[nsc % 2]
                            m = boundary(qt, kt)
                            s_sc.wait(v, nsc)
                            if m is not None:
                                s_msk.inc(v.tensor_tensor(sbank[:], sbank[:],
                                                          amask[:, ts(m, 512)], ALU.add))
                            else:
                                s_msk.inc(v.engine_nop())
                            s_ph.wait(v, nsc)
                            s_pl.inc(v.tensor_sub(plb[nsc % 2][:], pf[nsc % 2][:],
                                                  phb[nsc % 2][:]))
                        s_pvk.wait(v, NSC_END[hq])
                        if hq >= 1:
                            s_tp.wait(v, hq)
                        last = None
                        for sub in range(4):
                            pvb = PB[2 + sub // 2]
                            col = (sub % 2) * 256
                            last = v.reciprocal(dnb[:, sub:sub + 1],
                                                pvb[:, col + 128:col + 129])
                        ch.cut(v, last)
                        for sub in range(4):
                            pvb = PB[2 + sub // 2]
                            col = (sub % 2) * 256
                            last = v.tensor_scalar_mul(atm[:, ts(sub)], pvb[:, col:col + 128],
                                                       dnb[:, sub:sub + 1])
                        s_at.inc(last)
                        s_hi.wait(v, hq + 1)
                        s_ev.inc(v.tensor_sub(aol[hq % 2][:], PB[4][:], aoh[hq % 2][:]))

            @blk.scalar
            def _(sc):
                nsc = 0
                for h in range(H):
                    for qt in (0, 1):
                        hq = 2 * h + qt
                        for z, kt in enumerate(KTS[qt]):
                            nsc += 1
                            sbank = PB[nsc % 2]
                            s_msk.wait(sc, nsc)
                            if nsc > 2:
                                s_pl.wait(sc, nsc - 2)
                                s_pvk.wait(sc, nsc - 2)
                            s_p.inc(sc.activation(pf[nsc % 2][:], sbank[:], ACTF.Exp,
                                                  bias=ckcol[:, kt:kt + 1], scale=SCALE))
                            s_p.wait(sc, nsc)
                            s_ph.inc(sc.activation(phb[nsc % 2][:], pf[nsc % 2][:],
                                                   ACTF.Copy))
                        s_tp.wait(sc, hq + 1)
                        if hq >= 2:
                            s_sp[hq % 2].wait(sc, 32 * ((hq - 2) // 2 + 1))
                        s_hi.inc(sc.activation(aoh[hq % 2][:], PB[4][:], ACTF.Copy))

    def oproj_phase(fh):
        with Phase(POOL) as P, ExitStack() as X:
            s_whi, s_wk, s_ps, s_oev = (P.sem() for _ in range(4))
            s_wld = [P.sem('s'), P.sem('s')]
            s_a = [P.sem('s'), P.sem('s')]
            s_sp = [P.sem('g'), P.sem('g')]
            sb = lambda nm, sh, dt: _sbuf(X, nm, sh, dt)
            wf32 = [sb(f"wf32{b}", [128, 1024], F32) for b in range(2)]
            w_hi = sb("w_hi", [128, NE * 1024], BF16)
            w_lo = sb("w_lo", [128, NE * 1024], BF16)
            a_hi = [sb(f"a_hi{b}", [128, E], BF16) for b in range(2)]
            a_lo = [sb(f"a_lo{b}", [128, E], BF16) for b in range(2)]
            ost = [sb(f"ost{b}", [128, 1024], F32) for b in range(2)]
            blk = X.enter_context(nc.Block())

            @blk.sync
            def _(sync):
                for j in range(NE):
                    if j >= 2:
                        s_wk.wait(sync, j - 1)
                    s_wld[j % 2].inc(sync.dma_start(out=wf32[j % 2][:],
                                                    in_=wT["woT"][ts(j), ts(fh, 1024)]), 16)
                for u in range(NTQ):
                    if u >= 2:
                        s_ps.wait(sync, 2 * (u - 1))
                    s_a[u % 2].inc(sync.dma_start(
                        out=a_hi[u % 2].rearrange("p (j c) -> p j c", c=128),
                        in_=fm3d(atf_hi, TQ, u * 128)), 16)
                    s_a[u % 2].inc(sync.dma_start(
                        out=a_lo[u % 2].rearrange("p (j c) -> p j c", c=128),
                        in_=fm3d(atf_lo, TQ, u * 128)), 16)

            @blk.gpsimd
            def _(g):
                for u in range(NTQ):
                    s_oev.wait(g, 2 * (u + 1))
                    s_sp[u % 2].inc(g.dma_start(out=aor[ts(u), ts(fh, 1024)],
                                                in_=ost[u % 2][:]), 16)
                s_sp[0].wait(g, 16 * (NTQ // 2))
                s_sp[1].wait(g, 16 * (NTQ // 2))

            @blk.scalar
            def _(sc):
                for j in range(NE):
                    s_wld[j % 2].wait(sc, 16 * (j // 2 + 1))
                    s_whi.inc(sc.activation(w_hi[:, ts(j, 1024)], wf32[j % 2][:],
                                            ACTF.Copy))
                for u in range(NTQ):
                    for fc in range(2):
                        s_ps.wait(sc, 2 * u + fc + 1)
                        if u >= 2:
                            s_sp[u % 2].wait(sc, 16 * ((u - 2) // 2 + 1))
                        s_oev.inc(sc.activation(ost[u % 2][:, ts(fc, 512)], PB[fc][:],
                                                ACTF.Copy))

            @blk.vector
            def _(v):
                for j in range(NE):
                    s_whi.wait(v, j + 1)
                    s_wk.inc(v.tensor_sub(w_lo[:, ts(j, 1024)], wf32[j % 2][:],
                                          w_hi[:, ts(j, 1024)]))

            @blk.tensor
            def _(t):
                s_wk.wait(t, NE)
                for u in range(NTQ):
                    s_a[u % 2].wait(t, 32 * (u // 2 + 1))
                    for fc in range(2):
                        idx = 2 * u + fc
                        if idx >= 2:
                            s_oev.wait(t, idx - 1)
                        b = PB[fc]
                        for j in range(NE):
                            ah = a_hi[u % 2][:, ts(j)]
                            al = a_lo[u % 2][:, ts(j)]
                            bh = w_hi[:, j * 1024 + fc * 512: j * 1024 + fc * 512 + 512]
                            bl = w_lo[:, j * 1024 + fc * 512: j * 1024 + fc * 512 + 512]
                            t.matmul(b[:], ah, bh, start=(j == 0), stop=False)
                            t.matmul(b[:], ah, bl, start=False, stop=False)
                            mm = t.matmul(b[:], al, bh, start=False, stop=(j == NE - 1))
                        s_ps.inc(mm)

    if en("F"):
        oproj_phase(0)
        oproj_phase(1)

    if en("R"):
        with Phase(POOL) as P, ExitStack() as X:
            s_sq, s_var, s_sd, s_x2 = (P.sem() for _ in range(4))
            s_ar = [P.sem('s'), P.sem('s')]
            s_xq = [P.sem('s'), P.sem('s')]
            s_sp = [P.sem('g'), P.sem('g')]
            sb = lambda nm, sh, dt: _sbuf(X, nm, sh, dt)
            ar = [sb(f"ar{b}", [128, E], F32) for b in range(2)]
            xq = [sb(f"xq{b}", [128, E], F32) for b in range(2)]
            scrap = sb("scrap", [128, E], F32)
            lstat = [sb(f"lstat{b}", [128, 8], F32) for b in range(2)]
            x2t = [sb(f"x2t{b}", [128, E], F32) for b in range(2)]
            blk = X.enter_context(nc.Block())

            @blk.sync
            def _(sync):
                for u in range(NTQ):
                    if u >= 2:
                        s_x2.wait(sync, u - 1)
                    s_ar[u % 2].inc(sync.dma_start(out=ar[u % 2][:], in_=aor[ts(u), :]), 16)
                    s_xq[u % 2].inc(sync.dma_start(out=xq[u % 2][:], in_=xkv[ts(u + 8), :]), 16)

            @blk.gpsimd
            def _(g):
                for u in range(NTQ):
                    s_x2.wait(g, u + 1)
                    s_sp[u % 2].inc(g.dma_start(out=x2sp[ts(u), :], in_=x2t[u % 2][:]), 16)
                s_sp[0].wait(g, 16 * (NTQ // 2))
                s_sp[1].wait(g, 16 * (NTQ // 2))

            @blk.scalar
            def _(sc):
                for u in range(NTQ):
                    st = lstat[u % 2]
                    s_ar[u % 2].wait(sc, 16 * (u // 2 + 1))
                    s_sq.inc(sc.activation(scrap[:], ar[u % 2][:], ACTF.Square,
                                           accum_out=st[:, 1:2]))
                    s_var.wait(sc, u + 1)
                    s_sd.inc(sc.activation(st[:, 5:6], st[:, 4:5], ACTF.Sqrt))

            @blk.vector
            def _(v):
                ch = P.chain()
                for u in range(NTQ):
                    st = lstat[u % 2]
                    s_sq.wait(v, u + 1)
                    s_var.inc(ln_stats_dve(v, st, ar[u % 2][:], ch))
                    s_sd.wait(v, u + 1)
                    ch.cut(v, v.reciprocal(st[:, 6:7], st[:, 5:6]))
                    if u >= 2:
                        s_sp[u % 2].wait(v, 16 * ((u - 2) // 2 + 1))
                    ch.cut(v, v.tensor_scalar(x2t[u % 2][:], ar[u % 2][:], st[:, 3:4],
                                              st[:, 6:7], ALU.subtract, ALU.mult))
                    s_xq[u % 2].wait(v, 16 * (u // 2 + 1))
                    s_x2.inc(v.tensor_add(x2t[u % 2][:], x2t[u % 2][:], xq[u % 2][:]))

    if en("G"):
        ln_fm_phase(x2sp, x2lt_hi, x2lt_lo, NTQ, TQ)

    if en("Z"):
        for tq in range(4):
            with Phase(POOL) as P, ExitStack() as X:
                s_gps, s_sil, s_fu, s_fhi, s_flo = (P.sem() for _ in range(5))
                s_dps, s_ffd, s_ftp, s_fev, s_sq, s_var, s_sd, s_add = (
                    P.sem() for _ in range(8))
                s_x2l, s_x2r = P.sem('s'), P.sem('s')
                s_ost = P.sem('g')
                s_wg = [P.sem('g'), P.sem('g')]
                s_wd = [P.sem('g'), P.sem('g')]
                sb = lambda nm, sh, dt: _sbuf(X, nm, sh, dt)
                xt2h = sb("xt2h", [128, NE * 256], BF16)
                xt2l = sb("xt2l", [128, NE * 256], BF16)
                fush = sb("fush", [128, 64 * 256], BF16)
                fusl = sb("fusl", [128, 64 * 256], BF16)
                wgb = [sb(f"wgb{b}", [128, E], BF16) for b in range(2)]
                wub = [sb(f"wub{b}", [128, E], BF16) for b in range(2)]
                wdb = [sb(f"wdb{b}", [128, E], BF16) for b in range(2)]
                silb = [sb(f"silb{b}", [128, 256], F32) for b in range(2)]
                fub = [sb(f"fub{b}", [128, 256], F32) for b in range(2)]
                ffd = sb("ffd", [128, 16 * 256], F32)
                ftm = sb("ftm", [128, 2 * E], F32)
                x2r = sb("x2r", [128, E], F32)
                scrap = sb("scrapq", [128, E], F32)
                lstat = sb("lstatq", [128, 8], F32)
                otile = sb("otile", [128, E], F32)
                blk = X.enter_context(nc.Block())

                @blk.sync
                def _(sync):
                    s_x2l.inc(sync.dma_start(
                        out=xt2h.rearrange("p (j c) -> p j c", c=256),
                        in_=fm3d(x2lt_hi, TQ, tq * 256, width=256)), 16)
                    s_x2l.inc(sync.dma_start(
                        out=xt2l.rearrange("p (j c) -> p j c", c=256),
                        in_=fm3d(x2lt_lo, TQ, tq * 256, width=256)), 16)
                    for sub in range(2):
                        if sub == 1:
                            s_add.wait(sync, 1)
                        s_x2r.inc(sync.dma_start(out=x2r[:],
                                                 in_=x2sp[ts(2 * tq + sub), :]), 16)

                @blk.gpsimd
                def _(g):
                    for c in range(64):
                        if c >= 2:
                            s_gps.wait(g, c - 1)
                        s_wg[c % 2].inc(g.dma_start(
                            out=wgb[c % 2].rearrange("p (j c2) -> p j c2", c2=128),
                            in_=bass.AP(wg8.tensor, c * 128,
                                        [[FI, 128], [128 * FI, NE], [1, 128]])), 16)
                        s_wg[c % 2].inc(g.dma_start(
                            out=wub[c % 2].rearrange("p (j c2) -> p j c2", c2=128),
                            in_=bass.AP(wu8.tensor, c * 128,
                                        [[FI, 128], [128 * FI, NE], [1, 128]])), 16)
                    for o in range(64):
                        if o >= 2:
                            s_dps.wait(g, o - 1)
                        s_wd[o % 2].inc(g.dma_start(out=wdb[o % 2][:], in_=wd8[ts(o), :]), 16)
                    for sub in range(2):
                        s_add.wait(g, sub + 1)
                        s_ost.inc(g.dma_start(out=out[ts(2 * tq + sub), :],
                                              in_=otile[:]), 16)
                        s_ost.wait(g, 16 * (sub + 1))

                @blk.tensor
                def _(t):
                    s_x2l.wait(t, 32)
                    for c in range(64):
                        s_wg[c % 2].wait(t, 32 * (c // 2 + 1))
                        if c >= 2:
                            s_fu.wait(t, c - 1)
                        b = PB[c % 2]
                        for j in range(NE):
                            wgj = wgb[c % 2][:, ts(j)]
                            wuj = wub[c % 2][:, ts(j)]
                            xh = xt2h[:, j * 256: j * 256 + 256]
                            xl2 = xt2l[:, j * 256: j * 256 + 256]
                            t.matmul(b[:, 0:256], wgj, xh, start=(j == 0), stop=False)
                            t.matmul(b[:, 0:256], wgj, xl2, start=False, stop=False)
                            t.matmul(b[:, 256:512], wuj, xh, start=False, stop=False)
                            mm = t.matmul(b[:, 256:512], wuj, xl2, start=False,
                                          stop=(j == NE - 1))
                        s_gps.inc(mm)
                    for o in range(64):
                        s_wd[o % 2].wait(t, 16 * (o // 2 + 1))
                        if o == 0:
                            s_fu.wait(t, 64)
                        s_flo.wait(t, o + 1)
                        for hc in range(16):
                            b = PB[hc // 2]
                            col = (hc % 2) * 256
                            fh = fush[:, o * 256: o * 256 + 256]
                            fl = fusl[:, o * 256: o * 256 + 256]
                            wdj = wdb[o % 2][:, ts(hc)]
                            t.matmul(b[:, col:col + 256], wdj, fh,
                                     start=(o == 0 and hc % 2 == 0), stop=False)
                            mm = t.matmul(b[:, col:col + 256], wdj, fl, start=False,
                                          stop=(o == 63 and hc % 2 == 1))
                        s_dps.inc(mm)
                    s_ffd.wait(t, 16)
                    for j in range(32):
                        sub, hc = divmod(j, 16)
                        mm = t.matmul(PB[j // 4][:, ts(j % 4)],
                                      ffd[:, hc * 256 + sub * 128: hc * 256 + sub * 128 + 128],
                                      ident[:], is_transpose=True,
                                      start=(j % 4 == 0), stop=(j % 4 == 3))
                        if j % 4 == 3:
                            s_ftp.inc(mm)

                @blk.scalar
                def _(sc):
                    for c in range(64):
                        s_gps.wait(sc, c + 1)
                        if c >= 2:
                            s_fu.wait(sc, c - 1)
                        b = PB[c % 2]
                        s_sil.inc(sc.activation(silb[c % 2][:], b[:, 0:256], ACTF.Silu,
                                                scale=sgate_p[:, c:c + 1]))
                        s_fu.wait(sc, c + 1)
                        s_fhi.inc(sc.activation(fush[:, ts(c, 256)], fub[c % 2][:],
                                                ACTF.Copy))
                    for g8 in range(8):
                        s_ftp.wait(sc, g8 + 1)
                        j0 = 4 * g8
                        sub = j0 // 16
                        s_fev.inc(sc.activation(
                            ftm[:, sub * E + (j0 % 16) * 128: sub * E + (j0 % 16) * 128 + 512],
                            PB[g8][:], ACTF.Copy))
                    for sub in range(2):
                        s_fev.wait(sc, 4 * (sub + 1))
                        s_sq.inc(sc.activation(scrap[:], ftm[:, ts(sub, E)], ACTF.Square,
                                               accum_out=lstat[:, 1:2]))
                        s_var.wait(sc, sub + 1)
                        s_sd.inc(sc.activation(lstat[:, 5:6], lstat[:, 4:5], ACTF.Sqrt))

                @blk.vector
                def _(v):
                    ch = P.chain()
                    for c in range(64):
                        s_sil.wait(v, c + 1)
                        b = PB[c % 2]
                        ch.cut(v, v.tensor_scalar_mul(silb[c % 2][:], silb[c % 2][:],
                                                      sup_p[:, c:c + 1]))
                        if c >= 2:
                            s_fhi.wait(v, c - 1)
                        s_fu.inc(v.tensor_tensor(fub[c % 2][:], silb[c % 2][:],
                                                 b[:, 256:512], ALU.mult))
                        s_fhi.wait(v, c + 1)
                        s_flo.inc(v.tensor_sub(fusl[:, ts(c, 256)], fub[c % 2][:],
                                               fush[:, ts(c, 256)]))
                    s_dps.wait(v, 64)
                    for hc in range(16):
                        b = PB[hc // 2]
                        col = (hc % 2) * 256
                        s_ffd.inc(v.tensor_scalar_mul(ffd[:, ts(hc, 256)],
                                                      b[:, col:col + 256],
                                                      sdown_p[:, hc:hc + 1]))
                    for sub in range(2):
                        s_sq.wait(v, sub + 1)
                        s_var.inc(ln_stats_dve(v, lstat, ftm[:, ts(sub, E)], ch))
                        s_sd.wait(v, sub + 1)
                        ch.cut(v, v.reciprocal(lstat[:, 6:7], lstat[:, 5:6]))
                        if sub == 1:
                            s_ost.wait(v, 16)
                        ch.cut(v, v.tensor_scalar(otile[:], ftm[:, ts(sub, E)],
                                                  lstat[:, 3:4], lstat[:, 6:7],
                                                  ALU.subtract, ALU.mult))
                        s_x2r.wait(v, 16 * (sub + 1))
                        s_add.inc(v.tensor_add(otile[:], otile[:], x2r[:]))

    return nc


def kernel(L0_attn_x, L0_attn_wq, L0_attn_wk, L0_attn_wv, L0_attn_wo,
           L0_ffn_w_gate, L0_ffn_w_up, L0_ffn_w_down,
           L0_ffn_s_gate, L0_ffn_s_up, L0_ffn_s_down):
    B = L0_attn_x.shape[0]
    shared = dict(
        wqT=np.ascontiguousarray(np.asarray(L0_attn_wq).T),
        wkT=np.ascontiguousarray(np.asarray(L0_attn_wk).T),
        wvT=np.ascontiguousarray(np.asarray(L0_attn_wv).T),
        woT=np.ascontiguousarray(np.asarray(L0_attn_wo).T),
        w_gate=np.ascontiguousarray(np.asarray(L0_ffn_w_gate)),
        w_up=np.ascontiguousarray(np.asarray(L0_ffn_w_up)),
        w_down=np.ascontiguousarray(np.asarray(L0_ffn_w_down)),
        s_gate=np.ascontiguousarray(np.asarray(L0_ffn_s_gate).reshape(1, FI)),
        s_up=np.ascontiguousarray(np.asarray(L0_ffn_s_up).reshape(1, FI)),
        s_down=np.ascontiguousarray(np.asarray(L0_ffn_s_down).reshape(1, E)),
    )
    x = np.asarray(L0_attn_x)
    in_maps = []
    for c in range(2 * B):
        p, hhalf = divmod(c, 2)
        seq = x[p]
        if hhalf == 0:
            x_kv = np.concatenate([np.zeros((TQ, E), np.float32), seq[0:TQ]], axis=0)
            pos_v = np.arange(-TQ, TQ, dtype=np.float32)
        else:
            x_kv = seq
            pos_v = np.arange(0, T, dtype=np.float32)
        in_maps.append(dict(shared, x_kv=np.ascontiguousarray(x_kv),
                            pos=np.ascontiguousarray(pos_v.reshape(1, T))))
    nc = build_nc()
    res = run_bass_kernel_spmd(nc, in_maps, core_ids=list(range(2 * B)))
    outp = np.zeros((B, T, E), np.float32)
    for c, r in enumerate(res.results):
        p, hhalf = divmod(c, 2)
        outp[p, hhalf * TQ:(hhalf + 1) * TQ] = r["out"]
    return outp
